# version 04 (final)
# baseline (speedup 1.0000x reference)
import contextlib
import numpy as np
import concourse.bass as bass
import concourse.mybir as mybir
from concourse.bass_utils import run_bass_kernel_spmd

F32 = mybir.dt.float32
BF16 = mybir.dt.bfloat16
I32 = mybir.dt.int32
AF = mybir.ActivationFunctionType
ALU = mybir.AluOpType
AX = mybir.AxisListType
PI = 3.141592653589793
B, L, D = 8, 4096, 1024
HID = 2048
EXP = 8
EPS32 = 1.1920929e-07

ENGS = ("sy", "pe", "dv", "ac", "gp")
T0 = ("sy", 0)


class P:
    KEYS = ("sy", "pe", "dv", "ac", "gp", "dsy", "dgp")

    def __init__(self, nc, es):
        self.nc = nc
        self.ops = {e: [] for e in ENGS}
        self.cnt = {k: 0 for k in self.KEYS}
        self.sem = {k: es.enter_context(nc.semaphore("sem_" + k)) for k in self.KEYS}

    def emit(self, eng, fn, inc=0, waits=(), key=None):
        key = key or eng
        w = tuple((k2, c2) for (k2, c2) in waits if c2 > 0)
        sem, mysem = self.sem, self.sem[key]

        def run(obj):
            for (k2, c2) in w:
                obj.wait_ge(sem[k2], c2)
            inst = fn(obj)
            if inc:
                inst.then_inc(mysem, inc)
        self.ops[eng].append(run)
        if inc:
            self.cnt[key] += inc
        return (key, self.cnt[key])

    def dma(self, eng, fn, waits=()):
        key = "d" + eng
        prev = self.cnt[key]
        return self.emit(eng, fn, inc=16, waits=list(waits) + [(key, prev)], key=key)

    def tok(self, key):
        return (key, self.cnt[key])

    def barrier(self):
        owncomp = {e: self.cnt[e] for e in ENGS}
        owndma = {e: self.cnt.get("d" + e, 0) for e in ENGS}
        targets = {e: self.cnt[e] + 1 for e in ENGS}
        sem = self.sem
        for e in ENGS:
            self.cnt[e] += 1

            def run(obj, e=e):
                if owncomp[e] > 0:
                    obj.wait_ge(sem[e], owncomp[e])
                if owndma[e] > 0:
                    obj.wait_ge(sem["d" + e], owndma[e])
                obj.sem_inc(sem[e], 1)
                for e2 in ENGS:
                    if e2 != e:
                        obj.wait_ge(sem[e2], targets[e2])
            self.ops[e].append(run)
        return ("pe", targets["pe"])

    def finalize(self, block):
        ops = self.ops

        @block.sync
        def _(o):
            for f in ops["sy"]:
                f(o)

        @block.tensor
        def _(o):
            for f in ops["pe"]:
                f(o)

        @block.vector
        def _(o):
            for f in ops["dv"]:
                f(o)

        @block.scalar
        def _(o):
            for f in ops["ac"]:
                f(o)

        @block.gpsimd
        def _(o):
            for f in ops["gp"]:
                f(o)


def APX(handle, off, dims):
    return bass.AP(handle, off, [list(d) for d in dims])


def sb(nc, es, name, shape, dt=F32):
    return es.enter_context(nc.sbuf_tensor(name, list(shape), dt))


def ps(nc, es, name, shape, dt=F32):
    return es.enter_context(nc.psum_tensor(name, list(shape), dt))


def emit_cast_phase(p, V, casts, start=T0):
    stage = V["castf"]
    stg_b = V["castb"]
    free_f = [start, start]
    free_b = [start, start]
    i = 0
    for (src, dst, total) in casts:
        off = 0
        while off < total:
            n = min(128 * 2048, total - off)
            cols = n // 128
            par = i % 2
            i += 1
            ld = p.dma("sy", lambda o, src=src, off=off, cols=cols, par=par: o.dma_start(
                out=stage[par][:, 0:cols], in_=APX(src, off, [[cols, 128], [1, cols]])),
                waits=[free_f[par]])
            cv = p.emit("dv", lambda o, cols=cols, par=par: o.tensor_copy(
                out=stg_b[par][:, 0:cols], in_=stage[par][:, 0:cols]),
                inc=1, waits=[ld, free_b[par]])
            free_f[par] = cv
            st = p.dma("gp", lambda o, dst=dst, off=off, cols=cols, par=par: o.dma_start(
                out=APX(dst, off, [[cols, 128], [1, cols]]), in_=stg_b[par][:, 0:cols]),
                waits=[cv])
            free_b[par] = st
            off += n
    return p.barrier()


def emit_ln(p, V, x_ap, z_out_ap, dep, n=1024):
    st = V["lnst"]
    sq = V["lnsq"]
    t1 = p.emit("dv", lambda o: o.reduce_sum(out=st[:, 0:1], in_=x_ap, axis=AX.X),
                inc=1, waits=dep)
    t2 = p.emit("ac", lambda o: o.activation(out=sq[:, 0:n], in_=x_ap, func=AF.Square,
                                             accum_out=st[:, 1:2]), inc=1, waits=dep + [V["lnsq_free"]])
    t3 = p.emit("dv", lambda o: o.tensor_single_scalar(out=st[:, 2:3], in_=st[:, 0:1],
                                                       scalar=1.0 / n, op=ALU.mult), inc=1, waits=[t1])
    t4 = p.emit("dv", lambda o: o.tensor_mul(out=st[:, 4:5], in0=st[:, 2:3], in1=st[:, 2:3]),
                inc=1, waits=[t3])
    t5 = p.emit("dv", lambda o: o.tensor_single_scalar(out=st[:, 1:2], in_=st[:, 1:2],
                                                       scalar=1.0 / n, op=ALU.mult), inc=1, waits=[t2])
    t6 = p.emit("dv", lambda o: o.tensor_sub(out=st[:, 1:2], in0=st[:, 1:2], in1=st[:, 4:5]),
                inc=1, waits=[t4, t5])
    t7 = p.emit("ac", lambda o: o.activation(out=st[:, 4:5], in_=st[:, 1:2], func=AF.Ln,
                                             bias=V["c_lneps"][:, 0:1]), inc=1, waits=[t6])
    t8 = p.emit("ac", lambda o: o.activation(out=st[:, 3:4], in_=st[:, 4:5], func=AF.Exp,
                                             scale=-0.5), inc=1, waits=[t7])
    t9 = p.emit("dv", lambda o: o.tensor_scalar(out=z_out_ap, in0=x_ap,
                                                scalar1=st[:, 2:3], scalar2=st[:, 3:4],
                                                op0=ALU.subtract, op1=ALU.mult), inc=1, waits=[t8])
    V["lnsq_free"] = t9
    return t9


def emit_transpose(p, V, src_tm, z_cm, coloff, dep):
    pt = V["pt_bf"]
    last = None
    for ct in range(8):
        w = [dep] if last is None else [dep, last]
        tt = p.emit("pe", lambda o, ct=ct: o.transpose(
            out=pt[0:128, 0:128], in_=src_tm[:, ct * 128:(ct + 1) * 128],
            identity=V["ident_bf"][:]), inc=1, waits=w)
        last = p.emit("dv", lambda o, ct=ct: o.tensor_copy(
            out=z_cm[ct][:, coloff:coloff + 128], in_=pt[0:128, 0:128]), inc=1, waits=[tt])
    return last



def emit_l0ffn(p, V, dep_z):
    pb = V["pb"]
    z_cm = V["z_cm"]
    ng = V["ng"]
    lg, lgs, gst = V["lg"], V["lgs"], V["gst"]
    h_sb, y_sb = V["h_sb"], V["y_sb"]
    dep_gate = []
    for s in range(8):
        mm = None
        for ct in range(8):
            if ct == 0:
                w = [dep_z] + ([dep_gate[-1]] if dep_gate else [])
            else:
                w = []
            mm = p.emit("pe", lambda o, s=s, ct=ct: o.matmul(
                out=pb[4][0:128, 0:8], lhsT=z_cm[ct][:, s * 128:(s + 1) * 128],
                rhs=V["gatewT_sb"][ct], start=(ct == 0), stop=(ct == 7)),
                inc=(1 if ct == 7 else 0), waits=w)
        t1 = p.emit("dv", lambda o: o.tensor_add(out=lg[:], in0=pb[4][0:128, 0:8],
                                                 in1=V["gb_bc"][:, 0:8]), inc=1, waits=[mm])
        t2 = p.emit("ac", lambda o: o.activation(out=lgs[:], in_=lg[:], func=AF.Square,
                                                 accum_out=gst[:, 0:1]), inc=1, waits=[t1])
        t3 = p.emit("dv", lambda o: o.tensor_scalar(out=gst[:, 0:1], in0=gst[:, 0:1],
                                                    scalar1=0.125, scalar2=EPS32,
                                                    op0=ALU.mult, op1=ALU.add), inc=1, waits=[t2])
        t4 = p.emit("ac", lambda o: o.activation(out=gst[:, 1:2], in_=gst[:, 0:1], func=AF.Ln),
                    inc=1, waits=[t3])
        t5 = p.emit("ac", lambda o: o.activation(out=gst[:, 0:1], in_=gst[:, 1:2], func=AF.Exp,
                                                 scale=-0.5), inc=1, waits=[t4])
        t6 = p.emit("dv", lambda o: o.tensor_single_scalar(out=lg[:], in_=lg[:],
                                                           scalar=gst[:, 0:1], op=ALU.mult),
                    inc=1, waits=[t5])
        t7 = p.emit("dv", lambda o: o.tensor_mul(out=lg[:], in0=lg[:], in1=V["rmsw_bc"][:, 0:8]),
                    inc=1, waits=[t6])
        t8 = p.emit("dv", lambda o: o.reduce_max(out=gst[:, 2:3], in_=lg[:], axis=AX.X),
                    inc=1, waits=[t7])
        t9 = p.emit("dv", lambda o: o.tensor_single_scalar(out=gst[:, 3:4], in_=gst[:, 2:3],
                                                           scalar=-2.0, op=ALU.mult), inc=1, waits=[t8])
        ta = p.emit("ac", lambda o: o.activation(out=lgs[:], in_=lg[:], func=AF.Exp,
                                                 bias=gst[:, 3:4], scale=2.0), inc=1, waits=[t9])
        tb = p.emit("dv", lambda o: o.reduce_sum(out=gst[:, 4:5], in_=lgs[:], axis=AX.X),
                    inc=1, waits=[ta])
        tc = p.emit("dv", lambda o: o.tensor_scalar(out=gst[:, 5:6], in0=gst[:, 4:5],
                                                    scalar1=1e-6, scalar2=1.0,
                                                    op0=ALU.mult, op1=ALU.add), inc=1, waits=[tb])
        td = p.emit("dv", lambda o, s=s: o.reciprocal(out=ng[:, s:s + 1], in_=gst[:, 5:6]),
                    inc=1, waits=[tc])
        dep_gate.append(td)
    w1sb, w2sb = V["w1e_sb"], V["w2e_sb"]
    w1_free = V.get("w1_free", T0)
    w2_free = V.get("w2_free", T0)
    y_use = [V.get("y_init", T0)] * 8
    y_written = [False] * 16
    hv_by_sc = [T0, T0]
    yev_by_par = [T0, T0]
    for e in range(EXP):
        ld1 = p.dma("sy", lambda o, e=e: o.dma_start(
            out=w1sb[:, 0:16384],
            in_=APX(V["w1bf_hbm"], e * D * HID, [[HID, 128], [HID * 128, 8], [1, HID]])),
            waits=[w1_free])
        ld2 = p.dma("sy", lambda o, e=e: o.dma_start(
            out=w2sb[:, 0:16384],
            in_=APX(V["w2bf_hbm"], e * HID * D, [[D, 128], [D * 128, 16], [1, D]])),
            waits=[w2_free])
        h_deps = []
        mm_h = None
        for ht in range(16):
            for sc in range(2):
                for ct in range(8):
                    w = []
                    if ct == 0:
                        w = [hv_by_sc[sc]]
                        if ht == 0 and sc == 0:
                            w += [ld1, dep_z, V.get("h_free", T0)]
                    mm_h = p.emit("pe", lambda o, ht=ht, sc=sc, ct=ct: o.matmul(
                        out=pb[sc][0:128, 0:512],
                        lhsT=w1sb[:, ct * HID + ht * 128: ct * HID + ht * 128 + 128],
                        rhs=z_cm[ct][:, sc * 512:(sc + 1) * 512],
                        start=(ct == 0), stop=(ct == 7)),
                        inc=(1 if ct == 7 else 0), waits=w)
                hv = p.emit("ac", lambda o, e=e, ht=ht, sc=sc: o.activation(
                    out=h_sb[ht][:, sc * 512:(sc + 1) * 512], in_=pb[sc][0:128, 0:512],
                    func=AF.Gelu, bias=V["b1r_sb"][:, e * 16 + ht: e * 16 + ht + 1]),
                    inc=1, waits=[mm_h])
                hv_by_sc[sc] = hv
                h_deps.append(hv)
        w1_free = mm_h
        dep_h = h_deps[-1]
        mm_y_last = None
        for s in range(8):
            tv = None
            for dc in range(2):
                idx = s * 2 + dc
                par = idx % 2
                mm = None
                for ht in range(16):
                    w = []
                    if ht == 0:
                        w = [dep_h, ld2, yev_by_par[par]]
                        if dc == 0:
                            w.append(y_use[s])
                    mm = p.emit("pe", lambda o, s=s, dc=dc, ht=ht, par=par: o.matmul(
                        out=pb[2 + par][0:128, 0:512],
                        lhsT=h_sb[ht][:, s * 128:(s + 1) * 128],
                        rhs=w2sb[:, ht * D + dc * 512: ht * D + dc * 512 + 512],
                        start=(ht == 0), stop=(ht == 15)),
                        inc=(1 if ht == 15 else 0), waits=w)
                mm_y_last = mm
                if not y_written[idx]:
                    tv = p.emit("dv", lambda o, s=s, dc=dc, par=par: o.tensor_copy(
                        out=y_sb[s][:, dc * 512:(dc + 1) * 512],
                        in_=pb[2 + par][0:128, 0:512]), inc=1, waits=[mm])
                    y_written[idx] = True
                else:
                    tv = p.emit("dv", lambda o, s=s, dc=dc, par=par: o.tensor_add(
                        out=y_sb[s][:, dc * 512:(dc + 1) * 512],
                        in0=y_sb[s][:, dc * 512:(dc + 1) * 512],
                        in1=pb[2 + par][0:128, 0:512]), inc=1, waits=[mm])
                yev_by_par[par] = tv
            y_use[s] = tv
        V["h_free"] = mm_y_last
        w2_free = mm_y_last
    V["w1_free"] = w1_free
    V["w2_free"] = w2_free
    return dep_gate[-1], y_use


def emit_l0_combine(p, V, s, x_ap, out_ap, dep_y, dep_g, dep_x):
    y_sb, ng = V["y_sb"], V["ng"]
    t1 = p.emit("dv", lambda o, s=s: o.tensor_add(out=y_sb[s][:, 0:1024],
                                                  in0=y_sb[s][:, 0:1024],
                                                  in1=V["b2s_bc"][:, 0:1024]),
                inc=1, waits=[dep_y, dep_g, V["b2s_tok"]])
    t2 = p.emit("dv", lambda o, s=s: o.tensor_single_scalar(
        out=y_sb[s][:, 0:1024], in_=y_sb[s][:, 0:1024], scalar=ng[:, s:s + 1],
        op=ALU.mult), inc=1, waits=[t1])
    t3 = p.emit("dv", lambda o, s=s: o.scalar_tensor_tensor(
        out=out_ap, in0=y_sb[s][:, 0:1024], scalar=0.5, in1=x_ap,
        op0=ALU.mult, op1=ALU.add), inc=1, waits=[t2, dep_x])
    return t3


def build_bcast_row(p, V, src_handle, off, n, coloff, dst_sb, waits=()):
    row = V["row1"]
    ld = p.dma("sy", lambda o: o.dma_start(out=row[0:1, coloff:coloff + n],
                                           in_=APX(src_handle, off, [[n, 1], [1, n]])),
               waits=waits)
    mm = p.emit("pe", lambda o: o.matmul(out=V["pb"][4][0:128, coloff:coloff + n],
                                         lhsT=V["ones1_f"][0:1, 0:128],
                                         rhs=row[0:1, coloff:coloff + n], start=True, stop=True),
                inc=1, waits=[ld, V.get("ones_tok", T0)])
    cv = p.emit("dv", lambda o: o.tensor_copy(out=dst_sb[:, 0:n],
                                              in_=V["pb"][4][0:128, coloff:coloff + n]),
                inc=1, waits=[mm])
    return cv



def build_A1(phases=3, sub=9):
    nc = bass.Bass("TRN2", target_bir_lowering=False)
    es = contextlib.ExitStack()
    x_in = nc.dram_tensor("x", [L * D], F32, kind="ExternalInput")
    w1f = nc.dram_tensor("w1f", [EXP * D * HID], F32, kind="ExternalInput")
    w2f = nc.dram_tensor("w2f", [EXP * HID * D], F32, kind="ExternalInput")
    gwTf = nc.dram_tensor("gwTf", [D * 8], F32, kind="ExternalInput")
    ipwf = nc.dram_tensor("ipwf", [D * 3072], F32, kind="ExternalInput")
    gwhf = nc.dram_tensor("gwhf", [2 * D * D], F32, kind="ExternalInput")
    b1r = nc.dram_tensor("b1r", [128 * 128], F32, kind="ExternalInput")
    b2 = nc.dram_tensor("b2", [8 * D], F32, kind="ExternalInput")
    gb0 = nc.dram_tensor("gb0", [8], F32, kind="ExternalInput")
    rmsw = nc.dram_tensor("rmsw", [8], F32, kind="ExternalInput")
    ipb = nc.dram_tensor("ipb", [128 * 24], F32, kind="ExternalInput")
    cbv = nc.dram_tensor("cbv", [D * 2 * 3], F32, kind="ExternalInput")
    cbb = nc.dram_tensor("cbb", [D * 2], F32, kind="ExternalInput")
    gbh = nc.dram_tensor("gbh", [D * 2], F32, kind="ExternalInput")
    identf = nc.dram_tensor("identf", [128 * 128], F32, kind="ExternalInput")
    x1_out = nc.dram_tensor("x1", [L * D], F32, kind="ExternalOutput")
    v_out = nc.dram_tensor("v0", [D * L], BF16, kind="ExternalOutput")
    g1_out = nc.dram_tensor("g1", [D * L], F32, kind="ExternalOutput")
    g2_out = nc.dram_tensor("g2", [D * L], F32, kind="ExternalOutput")
    w1bf = nc.dram_tensor("w1bf", [EXP * D * HID], BF16, kind="Internal")
    w2bf = nc.dram_tensor("w2bf", [EXP * HID * D], BF16, kind="Internal")
    gwbf = nc.dram_tensor("gwbf", [D * 8], BF16, kind="Internal")
    ipwbf = nc.dram_tensor("ipwbf", [D * 3072], BF16, kind="Internal")
    gwhbf = nc.dram_tensor("gwhbf", [2 * D * D], BF16, kind="Internal")
    c_scr = nc.dram_tensor("c_scr", [2 * D * L], F32, kind="Internal")
    xc_scr = nc.dram_tensor("xc_scr", [2 * D * L], F32, kind="Internal")

    with es:
        V = {}
        p = P(nc, es)
        pb = [ps(nc, es, "pb%d" % i, (128, 512), F32) for i in range(7)]
        V["pb"] = pb
        V["pt_bf"] = es.enter_context(nc.psum_tensor("ptb", [128, 128], BF16))
        w1e_sb = sb(nc, es, "w1e", (128, 16384), BF16)
        w2e_sb = sb(nc, es, "w2e", (128, 16384), BF16)
        poolH = sb(nc, es, "poolH", (128, 16384), BF16)
        poolS = sb(nc, es, "poolS", (128, 8320), F32)
        poolZ = sb(nc, es, "poolZ", (128, 8192), BF16)
        zt = sb(nc, es, "zt", (128, 1024), BF16)
        V["w1e_sb"], V["w2e_sb"] = w1e_sb, w2e_sb
        V["h_sb"] = [poolH[:, i * 1024:(i + 1) * 1024] for i in range(16)]
        V["y_sb"] = [poolS[:, i * 1024:(i + 1) * 1024] for i in range(8)]
        V["z_cm"] = [poolZ[:, i * 1024:(i + 1) * 1024] for i in range(8)]
        V["castf"] = [poolS[:, 0:2048], poolS[:, 2048:4096]]
        V["castb"] = [poolZ[:, 0:2048], poolZ[:, 2048:4096]]
        V["lnst"] = sb(nc, es, "lnst", (128, 8), F32)
        V["lnsq"] = sb(nc, es, "lnsq", (128, 1024), F32)
        V["lnsq_free"] = T0
        V["lg"] = sb(nc, es, "lg", (128, 8), F32)
        V["lgs"] = sb(nc, es, "lgs", (128, 8), F32)
        V["gst"] = sb(nc, es, "gst", (128, 8), F32)
        V["ng"] = sb(nc, es, "ng", (128, 8), F32)
        V["gb_bc"] = sb(nc, es, "gbbc", (128, 8), F32)
        V["rmsw_bc"] = sb(nc, es, "rmswbc", (128, 8), F32)
        V["b2s_bc"] = sb(nc, es, "b2sbc", (128, 1024), F32)
        V["b1r_sb"] = sb(nc, es, "b1rsb", (128, 128), F32)
        V["ident_bf"] = sb(nc, es, "identb", (128, 128), BF16)
        V["ones1_f"] = sb(nc, es, "ones1", (1, 128), F32)
        ones8 = sb(nc, es, "ones8", (8, 1), F32)
        V["row1"] = sb(nc, es, "row1", (1, 1040), F32)
        V["c_lneps"] = sb(nc, es, "clneps", (128, 2), F32)
        gatewT_sb = sb(nc, es, "gatewT", (128, 64), BF16)
        V["gatewT_sb"] = [gatewT_sb[:, i * 8:(i + 1) * 8] for i in range(8)]
        ip_sb = sb(nc, es, "ipsb", (128, 2048), BF16)
        ipb_sb = sb(nc, es, "ipbsb", (128, 24), F32)
        ev_b = sb(nc, es, "evb", (128, 1024), BF16)
        ev_f = sb(nc, es, "evf", (128, 1024), F32)
        xrow = sb(nc, es, "xrow", (128, 2048), F32)
        cbv_sb = sb(nc, es, "cbvsb", (128, 48), F32)

        cbb_sb = sb(nc, es, "cbbsb", (128, 16), F32)
        gbh_sb = sb(nc, es, "gbhsb", (128, 16), F32)

        rowb = poolS
        blk = es.enter_context(nc.Block())
        p.dma("sy", lambda o: o.dma_start(out=cbv_sb[:], in_=APX(cbv, 0, [[6, 128], [768, 8], [1, 6]])))
        p.dma("sy", lambda o: o.dma_start(out=cbb_sb[:], in_=APX(cbb, 0, [[2, 128], [256, 8], [1, 2]])))
        p.dma("sy", lambda o: o.dma_start(out=gbh_sb[:], in_=APX(gbh, 0, [[2, 128], [256, 8], [1, 2]])))
        p.emit("dv", lambda o: o.memset(V["ones1_f"][:], 1.0), inc=1)
        p.emit("dv", lambda o: o.memset(ones8[:], 1.0), inc=1)
        V["ones_tok"] = p.emit("dv", lambda o: o.memset(V["c_lneps"][:, 0:1], 1e-5), inc=1)
        p.dma("sy", lambda o: o.dma_start(out=V["b1r_sb"][:], in_=APX(b1r, 0, [[128, 128], [1, 128]])))
        p.dma("sy", lambda o: o.dma_start(out=ipb_sb[:], in_=APX(ipb, 0, [[24, 128], [1, 24]])))
        idl = p.dma("sy", lambda o: o.dma_start(out=V["lnsq"][:, 0:128],
                                                in_=APX(identf, 0, [[128, 128], [1, 128]])))
        p.emit("dv", lambda o: o.tensor_copy(out=V["ident_bf"][:], in_=V["lnsq"][:, 0:128]),
               inc=1, waits=[idl])
        def subexit():
            p.barrier()
            p.dma("gp", lambda o: o.dma_start(out=APX(x1_out, 0, [[1024, 128], [1, 1024]]),
                                              in_=V["lnsq"][:, 0:1024]))
            p.barrier()
            p.finalize(blk)
            return nc
        if sub < 1:
            return subexit()
        build_bcast_row(p, V, gb0, 0, 8, 0, V["gb_bc"])
        if sub < 2:
            return subexit()
        build_bcast_row(p, V, rmsw, 0, 8, 8, V["rmsw_bc"])
        if sub < 3:
            return subexit()
        l1 = p.dma("sy", lambda o: o.dma_start(out=rowb[0:1, 0:8192], in_=APX(b2, 0, [[8192, 1], [1, 8192]])))
        if True:
            a1 = p.emit("dv", lambda o: o.tensor_add(out=rowb[0:1, 0:4096], in0=rowb[0:1, 0:4096],
                                                 in1=rowb[0:1, 4096:8192]), inc=1, waits=[l1])
            a2 = p.emit("dv", lambda o: o.tensor_add(out=rowb[0:1, 0:2048], in0=rowb[0:1, 0:2048],
                                                 in1=rowb[0:1, 2048:4096]), inc=1, waits=[a1])
            a3 = p.emit("dv", lambda o: o.tensor_add(out=rowb[0:1, 0:1024], in0=rowb[0:1, 0:1024],
                                                 in1=rowb[0:1, 1024:2048]), inc=1, waits=[a2])
        if True:
            m3 = p.emit("pe", lambda o: o.matmul(out=pb[0][0:128, 0:512], lhsT=V["ones1_f"][0:1, 0:128],
                                                 rhs=rowb[0:1, 0:512], start=True, stop=True),
                        waits=[a3, V["ones_tok"]])
            m4 = p.emit("pe", lambda o: o.matmul(out=pb[1][0:128, 0:512], lhsT=V["ones1_f"][0:1, 0:128],
                                                 rhs=rowb[0:1, 512:1024], start=True, stop=True),
                        inc=1, waits=[])
        if True:
            p.emit("dv", lambda o: o.tensor_copy(out=V["b2s_bc"][:, 0:512], in_=pb[0][0:128, 0:512]),
                   inc=1, waits=[m4])
            b2b = p.emit("dv", lambda o: o.tensor_copy(out=V["b2s_bc"][:, 512:1024], in_=pb[1][0:128, 0:512]),
                         inc=1, waits=[m4])
        V["b2s_tok"] = b2b
        if sub < 4:
            return subexit()
        V["w1bf_hbm"], V["w2bf_hbm"] = w1bf, w2bf
        emit_cast_phase(p, V, [
            (w1f, w1bf, EXP * D * HID), (w2f, w2bf, EXP * HID * D),
            (gwTf, gwbf, D * 8), (ipwf, ipwbf, D * 3072), (gwhf, gwhbf, 2 * D * D)],
            start=m4)
        p.dma("sy", lambda o: o.dma_start(out=gatewT_sb[:, 0:64],
                                          in_=APX(gwbf, 0, [[8, 128], [8 * 128, 8], [1, 8]])))
        if sub < 5:
            return subexit()
        bar = p.barrier()

        nbig = 0 if phases == 0 else (1 if phases == 1 else 4)
        ip_free = [bar, bar]
        ipev_by_sc = [bar, bar]
        evst_free = [bar, bar]
        for bc in range(nbig):
            tok0 = bc * 1024
            xld_free = [[bar], [bar]]
            dep_z = bar
            for s in range(8):
                par = s % 2
                xap = xrow[:, par * 1024:(par + 1) * 1024]
                ld = p.dma("sy", lambda o, s=s, par=par, tok0=tok0: o.dma_start(
                    out=xrow[:, par * 1024:(par + 1) * 1024],
                    in_=APX(x_in, (tok0 + s * 128) * D, [[D, 128], [1, D]])),
                    waits=xld_free[par])
                lnt = emit_ln(p, V, xap, zt[:, 0:1024], [ld])
                trt = emit_transpose(p, V, zt, V["z_cm"], s * 128, lnt)
                xld_free[par] = [trt]
                dep_z = trt
            dep_g, y_use = emit_l0ffn(p, V, dep_z)
            dep_z2 = dep_z
            for s in range(8):
                par = s % 2
                xap = xrow[:, par * 1024:(par + 1) * 1024]
                ld = p.dma("sy", lambda o, s=s, par=par, tok0=tok0: o.dma_start(
                    out=xrow[:, par * 1024:(par + 1) * 1024],
                    in_=APX(x_in, (tok0 + s * 128) * D, [[D, 128], [1, D]])),
                    waits=xld_free[par])
                cmb = emit_l0_combine(p, V, s, xap, xap, y_use[s], dep_g, ld)
                st = p.dma("gp", lambda o, s=s, par=par, tok0=tok0: o.dma_start(
                    out=APX(x1_out, (tok0 + s * 128) * D, [[D, 128], [1, D]]),
                    in_=xrow[:, par * 1024:(par + 1) * 1024]), waits=[cmb])
                lnt = emit_ln(p, V, xap, zt[:, 0:1024], [cmb])
                trt = emit_transpose(p, V, zt, V["z_cm"], s * 128, lnt)
                xld_free[par] = [st, trt]
                dep_z2 = trt
            for dt in range(24):
                par = dt % 2
                ld = p.dma("sy", lambda o, dt=dt, par=par: o.dma_start(
                    out=ip_sb[:, par * 1024:par * 1024 + 1024],
                    in_=APX(ipwbf, dt * 128, [[3072, 128], [3072 * 128, 8], [1, 128]])),
                    waits=[ip_free[par]])
                for sc in range(2):
                    mm = None
                    for ct in range(8):
                        w = [ld, dep_z2, ipev_by_sc[sc]] if ct == 0 else []
                        mm = p.emit("pe", lambda o, dt=dt, sc=sc, ct=ct, par=par: o.matmul(
                            out=pb[5 + sc][0:128, 0:512],
                            lhsT=ip_sb[:, par * 1024 + ct * 128: par * 1024 + ct * 128 + 128],
                            rhs=V["z_cm"][ct][:, sc * 512:(sc + 1) * 512],
                            start=(ct == 0), stop=(ct == 7)),
                            inc=(1 if ct == 7 else 0), waits=w)
                    ip_free[par] = mm
                    epar = sc
                    if dt < 8:
                        ev = p.emit("ac", lambda o, dt=dt, sc=sc, epar=epar: o.activation(
                            out=ev_b[:, epar * 512:(epar + 1) * 512], in_=pb[5 + sc][0:128, 0:512],
                            func=AF.Identity, bias=ipb_sb[:, dt:dt + 1]),
                            inc=1, waits=[mm, evst_free[epar]])
                        stv = p.dma("gp", lambda o, dt=dt, sc=sc, epar=epar, tok0=tok0: o.dma_start(
                            out=APX(v_out, dt * 128 * L + tok0 + sc * 512, [[L, 128], [1, 512]]),
                            in_=ev_b[:, epar * 512:(epar + 1) * 512]), waits=[ev])
                    else:
                        ev = p.emit("ac", lambda o, dt=dt, sc=sc, epar=epar: o.activation(
                            out=ev_f[:, epar * 512:(epar + 1) * 512], in_=pb[5 + sc][0:128, 0:512],
                            func=AF.Identity, bias=ipb_sb[:, dt:dt + 1]),
                            inc=1, waits=[mm, evst_free[epar]])
                        stv = p.dma("gp", lambda o, dt=dt, sc=sc, epar=epar, tok0=tok0: o.dma_start(
                            out=APX(c_scr, (dt - 8) * 128 * L + tok0 + sc * 512, [[L, 128], [1, 512]]),
                            in_=ev_f[:, epar * 512:(epar + 1) * 512]), waits=[ev])
                    ipev_by_sc[sc] = ev
                    evst_free[epar] = stv
            bar = p.barrier()
            ip_free = [bar, bar]
            ipev_by_sc = [bar, bar]
            evst_free = [bar, bar]
            V["w1_free"] = bar
            V["w2_free"] = bar
            V["h_free"] = bar
            V["y_init"] = bar

        def xcbf(cg):
            return poolH[:, cg * 4096:(cg + 1) * 4096] if cg < 4 else \
                w2e_sb[:, (cg - 4) * 4096:(cg - 3) * 4096]
        crow_free = [bar]
        sg_by_par = [bar, bar]
        gw_free = bar
        evf_a = bar
        evf_b2 = bar
        for i in range(2 if phases >= 2 else 0):
            xc_done = []
            for cg in range(8):
                ms = p.emit("dv", lambda o: o.memset(poolS[:, 0:4098], 0.0), inc=1,
                            waits=crow_free)
                ld = p.dma("sy", lambda o, i=i, cg=cg: o.dma_start(
                    out=poolS[:, 1:4097],
                    in_=APX(c_scr, i * D * L + cg * 128 * L, [[L, 128], [1, L]])),
                    waits=[ms])
                t1 = p.emit("dv", lambda o, i=i, cg=cg: o.tensor_single_scalar(
                    out=poolS[:, 4100:8196], in_=poolS[:, 0:4096],
                    scalar=cbv_sb[:, cg * 6 + i * 3: cg * 6 + i * 3 + 1], op=ALU.mult),
                    inc=1, waits=[ld] + crow_free)
                t2 = p.emit("dv", lambda o, i=i, cg=cg: o.scalar_tensor_tensor(
                    out=poolS[:, 4100:8196], in0=poolS[:, 1:4097],
                    scalar=cbv_sb[:, cg * 6 + i * 3 + 1: cg * 6 + i * 3 + 2],
                    in1=poolS[:, 4100:8196], op0=ALU.mult, op1=ALU.add), inc=1, waits=[t1])
                t3 = p.emit("dv", lambda o, i=i, cg=cg: o.scalar_tensor_tensor(
                    out=poolS[:, 4100:8196], in0=poolS[:, 2:4098],
                    scalar=cbv_sb[:, cg * 6 + i * 3 + 2: cg * 6 + i * 3 + 3],
                    in1=poolS[:, 4100:8196], op0=ALU.mult, op1=ALU.add), inc=1, waits=[t2])
                t4 = p.emit("dv", lambda o, i=i, cg=cg: o.tensor_single_scalar(
                    out=poolS[:, 4100:8196], in_=poolS[:, 4100:8196],
                    scalar=cbb_sb[:, cg * 2 + i: cg * 2 + i + 1], op=ALU.add),
                    inc=1, waits=[t3])
                st = p.dma("gp", lambda o, i=i, cg=cg: o.dma_start(
                    out=APX(xc_scr, i * D * L + cg * 128 * L, [[L, 128], [1, L]]),
                    in_=poolS[:, 4100:8196]), waits=[t4])
                cvt = p.emit("dv", lambda o, cg=cg: o.tensor_copy(
                    out=xcbf(cg), in_=poolS[:, 4100:8196]), inc=1,
                    waits=[t4] + ([gw_free] if cg >= 4 else []))
                crow_free = [st, cvt]
                xc_done.append(cvt)
            dep_xc = xc_done[-1]
            for cg in range(8):
                gld = p.dma("sy", lambda o, i=i, cg=cg: o.dma_start(
                    out=w1e_sb[:, 0:1024],
                    in_=APX(gwhbf, i * D * D + cg * 128, [[D, 128], [D * 128, 8], [1, 128]])),
                    waits=[gw_free])
                for tch in range(8):
                    par = tch % 2
                    mm = None
                    for ct in range(8):
                        w = [gld, dep_xc, sg_by_par[par]] if ct == 0 else []
                        mm = p.emit("pe", lambda o, cg=cg, tch=tch, ct=ct, par=par: o.matmul(
                            out=pb[par][0:128, 0:512],
                            lhsT=w1e_sb[:, ct * 128:(ct + 1) * 128],
                            rhs=xcbf(ct)[:, tch * 512:tch * 512 + 512],
                            start=(ct == 0), stop=(ct == 7)),
                            inc=(1 if ct == 7 else 0), waits=w)
                    gw_free = mm
                    sg = p.emit("ac", lambda o, i=i, cg=cg, par=par: o.activation(
                        out=ev_f[:, 0:512], in_=pb[par][0:128, 0:512], func=AF.Sigmoid,
                        bias=gbh_sb[:, cg * 2 + i: cg * 2 + i + 1]),
                        inc=1, waits=[mm, evf_a])
                    sg_by_par[par] = sg
                    xl = p.dma("sy", lambda o, i=i, cg=cg, tch=tch: o.dma_start(
                        out=ev_f[:, 512:1024],
                        in_=APX(xc_scr, i * D * L + cg * 128 * L + tch * 512, [[L, 128], [1, 512]])),
                        waits=[evf_b2])
                    gm = p.emit("dv", lambda o: o.tensor_mul(out=ev_f[:, 0:512],
                                                             in0=ev_f[:, 0:512],
                                                             in1=ev_f[:, 512:1024]),
                                inc=1, waits=[sg, xl])
                    evf_b2 = gm
                    gout = g1_out if i == 0 else g2_out
                    stg = p.dma("gp", lambda o, gout=gout, cg=cg, tch=tch: o.dma_start(
                        out=APX(gout, cg * 128 * L + tch * 512, [[L, 128], [1, 512]]),
                        in_=ev_f[:, 0:512]), waits=[gm])
                    evf_a = stg
            bar2 = p.barrier()
            crow_free = [bar2]
            sg_by_par = [bar2, bar2]
            gw_free = bar2
            evf_a = bar2
            evf_b2 = bar2
        p.finalize(blk)
    return nc


def a1_inputs(Xb, W):
    return {
        "x": np.ascontiguousarray(Xb.reshape(-1)),
        "w1f": W["L0_ffn_w1"].reshape(-1),
        "w2f": W["L0_ffn_w2"].reshape(-1),
        "gwTf": np.ascontiguousarray(W["L0_ffn_gate_w"].T).reshape(-1),
        "ipwf": W["L0_attn_in_proj_w"].reshape(-1),
        "gwhf": W["L0_attn_gate_w"].reshape(-1),
        "b1r": np.ascontiguousarray(W["L0_ffn_b1"].reshape(8, 16, 128).transpose(2, 0, 1)).reshape(-1),
        "b2": W["L0_ffn_b2"].reshape(-1),
        "gb0": W["L0_ffn_gate_b"].reshape(-1),
        "rmsw": W["L0_ffn_rms_w"].reshape(-1),
        "ipb": np.ascontiguousarray(W["L0_attn_in_proj_b"].reshape(24, 128).T).reshape(-1),
        "cbv": np.ascontiguousarray(W["L0_attn_conv_w"][:, :, 0, :].transpose(2, 0, 1)).reshape(-1),
        "cbb": np.ascontiguousarray(W["L0_attn_conv_b"].T).reshape(-1),
        "gbh": np.ascontiguousarray(W["L0_attn_gate_b"].T).reshape(-1),
        "identf": np.eye(128, dtype=np.float32).reshape(-1),
    }



def build_A2():
    nc = bass.Bass("TRN2", target_bir_lowering=False)
    es = contextlib.ExitStack()
    vq = nc.dram_tensor("vq", [128 * 128 * 256], BF16, kind="ExternalInput")
    gt1 = nc.dram_tensor("gt1", [128 * 128 * 256], F32, kind="ExternalInput")
    gt2 = nc.dram_tensor("gt2", [128 * 128 * 256], F32, kind="ExternalInput")
    t7r = nc.dram_tensor("t7r", [7 * L], F32, kind="ExternalInput")
    t7n = nc.dram_tensor("t7n", [7 * L], F32, kind="ExternalInput")
    tgr = nc.dram_tensor("tgr", [128 * L], F32, kind="ExternalInput")
    tgn = nc.dram_tensor("tgn", [128 * L], F32, kind="ExternalInput")
    freqs = nc.dram_tensor("freqs", [2 * 7], F32, kind="ExternalInput")
    shvin = nc.dram_tensor("shvin", [7], F32, kind="ExternalInput")
    mw1 = nc.dram_tensor("mw1", [2 * 7 * 64], F32, kind="ExternalInput")
    mb1 = nc.dram_tensor("mb1", [2 * 64], F32, kind="ExternalInput")
    mw2 = nc.dram_tensor("mw2", [2 * 64 * 64], F32, kind="ExternalInput")
    mb2 = nc.dram_tensor("mb2", [2 * 64], F32, kind="ExternalInput")
    mw3 = nc.dram_tensor("mw3", [2 * 64 * 128], F32, kind="ExternalInput")
    mb3 = nc.dram_tensor("mb3", [2 * 128], F32, kind="ExternalInput")
    dec = nc.dram_tensor("dec", [2 * 128], F32, kind="ExternalInput")
    v2_out = nc.dram_tensor("v2", [128 * 128 * 256], BF16, kind="ExternalOutput")
    hp_hbm = nc.dram_tensor("hp", [2 * 128 * 4352], BF16, kind="Internal")

    with es:
        V = {}
        p = P(nc, es)
        pb = [ps(nc, es, "pb%d" % i, (128, 512), F32) for i in range(6)]
        hsb = sb(nc, es, "hsb", (128, 4096), F32)
        win = sb(nc, es, "win", (128, 4096), F32)
        hh1 = sb(nc, es, "hh1", (64, 4096), F32)
        hh2 = sb(nc, es, "hh2", (64, 4096), F32)
        enc = sb(nc, es, "enc", (7, 4096), F32)
        encw = sb(nc, es, "encw", (7, 4096), F32)
        enci = sb(nc, es, "enci", (7, 4096), I32)
        fv = sb(nc, es, "fv", (7, 2), F32)
        shv = sb(nc, es, "shv", (7, 1), F32)
        mwsb = sb(nc, es, "mwsb", (64, 192), F32)
        w1sb = sb(nc, es, "w1sb", (7, 64), F32)
        w2sb = sb(nc, es, "w2sb", (64, 64), F32)
        w3sb = sb(nc, es, "w3sb", (64, 128), F32)
        b1sb = sb(nc, es, "b1sb", (64, 2), F32)
        b2sb = sb(nc, es, "b2sb", (64, 2), F32)
        b3sb = sb(nc, es, "b3sb", (128, 2), F32)
        decsb = sb(nc, es, "decsb", (128, 2), F32)
        t7sb = sb(nc, es, "t7sb", (7, 4096), F32)
        tgsb = sb(nc, es, "tgsb", (128, 4096), F32)
        hpad = sb(nc, es, "hpad", (128, 4352), BF16)
        G = [sb(nc, es, "G%d" % i, (128, 4096), BF16) for i in range(2)]
        Vt = [sb(nc, es, "Vt%d" % i, (128, 256), BF16) for i in range(2)]
        gt = [sb(nc, es, "gtt%d" % i, (128, 256), F32) for i in range(2)]
        v1t = sb(nc, es, "v1t", (128, 256), BF16)
        v2t = [sb(nc, es, "v2t%d" % i, (128, 256), BF16) for i in range(2)]

        blk = es.enter_context(nc.Block())
        hp_done = []
        for i in range(2):
            tsrc = t7r if i == 0 else t7n
            tgsrc = tgr if i == 0 else tgn
            l1 = p.dma("sy", lambda o, tsrc=tsrc: o.dma_start(out=t7sb[:], in_=APX(tsrc, 0, [[L, 7], [1, L]])))
            l2 = p.dma("sy", lambda o, tgsrc=tgsrc: o.dma_start(out=tgsb[:], in_=APX(tgsrc, 0, [[L, 128], [1, L]])))
            l3 = p.dma("sy", lambda o, i=i: o.dma_start(out=fv[0:7, 0:1], in_=APX(freqs, i * 7, [[1, 7], [1, 1]])))
            l4 = p.dma("sy", lambda o: o.dma_start(out=shv[0:7, 0:1], in_=APX(shvin, 0, [[1, 7], [1, 1]])))
            l5 = p.dma("sy", lambda o, i=i: o.dma_start(out=w1sb[:], in_=APX(mw1, i * 448, [[64, 7], [1, 64]])))
            l6 = p.dma("sy", lambda o, i=i: o.dma_start(out=w2sb[:], in_=APX(mw2, i * 4096, [[64, 64], [1, 64]])))
            l7 = p.dma("sy", lambda o, i=i: o.dma_start(out=w3sb[:], in_=APX(mw3, i * 8192, [[128, 64], [1, 128]])))
            l8 = p.dma("sy", lambda o, i=i: o.dma_start(out=b1sb[:, i:i + 1], in_=APX(mb1, i * 64, [[1, 64], [1, 1]])))
            l9 = p.dma("sy", lambda o, i=i: o.dma_start(out=b2sb[:, i:i + 1], in_=APX(mb2, i * 64, [[1, 64], [1, 1]])))
            la = p.dma("sy", lambda o, i=i: o.dma_start(out=b3sb[:, i:i + 1], in_=APX(mb3, i * 128, [[1, 128], [1, 1]])))
            lb = p.dma("sy", lambda o, i=i: o.dma_start(out=decsb[:, i:i + 1], in_=APX(dec, i * 128, [[1, 128], [1, 1]])))
            t1 = p.emit("dv", lambda o: o.tensor_scalar(out=encw[:], in0=t7sb[:], scalar1=fv[:, 0:1],
                                                        scalar2=shv[:, 0:1], op0=ALU.mult, op1=ALU.add),
                        inc=1, waits=[l1, l3, l4])
            t2 = p.emit("dv", lambda o: o.tensor_copy(out=enci[:], in_=encw[:]), inc=1, waits=[t1])
            t3 = p.emit("dv", lambda o: o.tensor_copy(out=enc[:], in_=enci[:]), inc=1, waits=[t2])
            t4 = p.emit("dv", lambda o: o.tensor_sub(out=encw[:], in0=encw[:], in1=enc[:]), inc=1, waits=[t3])
            t5 = p.emit("dv", lambda o: o.tensor_single_scalar(out=encw[:], in_=encw[:],
                                                               scalar=2 * PI, op=ALU.mult),
                        inc=1, waits=[t4])
            t6 = p.emit("dv", lambda o: o.tensor_scalar(out=enc[:], in0=encw[:], scalar1=PI,
                                                        scalar2=2 * PI, op0=ALU.is_ge, op1=ALU.mult),
                        inc=1, waits=[t5])
            t7 = p.emit("dv", lambda o: o.tensor_sub(out=encw[:], in0=encw[:], in1=enc[:]), inc=1, waits=[t6])
            t8 = p.emit("dv", lambda o: o.tensor_scalar(out=enc[:], in0=encw[:], scalar1=-PI,
                                                        scalar2=2 * PI, op0=ALU.is_lt, op1=ALU.mult),
                        inc=1, waits=[t7])
            t9 = p.emit("dv", lambda o: o.tensor_add(out=encw[:], in0=encw[:], in1=enc[:]), inc=1, waits=[t8])
            s1 = p.emit("ac", lambda o: o.activation(out=enc[:], in_=encw[:], func=AF.Sin), inc=1, waits=[t9])
            s2 = p.emit("dv", lambda o: o.tensor_single_scalar(out=enc[:], in_=enc[:], scalar=-1.0,
                                                               op=ALU.mult), inc=1, waits=[s1])
            s3 = p.dma("gp", lambda o: o.dma_start(out=enc[6:7, :], in_=t7sb[6:7, :]), waits=[s2, l1])
            evs = [T0, T0]
            for q in range(8):
                mm1 = p.emit("pe", lambda o, q=q: o.matmul(
                    out=pb[q % 2][0:64, 0:512], lhsT=w1sb[:], rhs=enc[0:7, q * 512:(q + 1) * 512],
                    start=True, stop=True), inc=1,
                    waits=([l5, s3] if q == 0 else []) + [evs[q % 2]])
                evs[q % 2] = p.emit("ac", lambda o, q=q, i=i: o.activation(
                    out=hh1[:, q * 512:(q + 1) * 512], in_=pb[q % 2][0:64, 0:512], func=AF.Gelu,
                    bias=b1sb[:, i:i + 1]), inc=1, waits=[mm1, l8])
            hv1 = p.tok("ac")
            evs = [T0, T0]
            for q in range(8):
                mm2 = p.emit("pe", lambda o, q=q: o.matmul(
                    out=pb[2 + q % 2][0:64, 0:512], lhsT=w2sb[:], rhs=hh1[0:64, q * 512:(q + 1) * 512],
                    start=True, stop=True), inc=1,
                    waits=([l6, hv1] if q == 0 else []) + [evs[q % 2]])
                evs[q % 2] = p.emit("ac", lambda o, q=q, i=i: o.activation(
                    out=hh2[:, q * 512:(q + 1) * 512], in_=pb[2 + q % 2][0:64, 0:512], func=AF.Gelu,
                    bias=b2sb[:, i:i + 1]), inc=1, waits=[mm2, l9])
            hv2 = p.tok("ac")
            evs = [T0, T0]
            for q in range(8):
                mm3 = p.emit("pe", lambda o, q=q: o.matmul(
                    out=pb[4 + q % 2][0:128, 0:512], lhsT=w3sb[:], rhs=hh2[0:64, q * 512:(q + 1) * 512],
                    start=True, stop=True), inc=1,
                    waits=([l7, hv2] if q == 0 else []) + [evs[q % 2]])
                evs[q % 2] = p.emit("ac", lambda o, q=q, i=i: o.activation(
                    out=hsb[:, q * 512:(q + 1) * 512], in_=pb[4 + q % 2][0:128, 0:512], func=AF.Identity,
                    bias=b3sb[:, i:i + 1]), inc=1, waits=[mm3, la])
            hv3 = p.tok("ac")
            w1_ = p.emit("ac", lambda o, i=i: o.activation(out=decsb[:, i:i + 1], in_=decsb[:, i:i + 1],
                                                           func=AF.Abs), inc=1, waits=[lb])
            w2_ = p.emit("dv", lambda o, i=i: o.tensor_single_scalar(out=win[:], in_=tgsb[:],
                                                                     scalar=decsb[:, i:i + 1],
                                                                     op=ALU.mult), inc=1, waits=[l2, w1_])
            w3_ = p.emit("ac", lambda o: o.activation(out=win[:], in_=win[:], func=AF.Exp, scale=-1.0),
                         inc=1, waits=[w2_])
            hm = p.emit("dv", lambda o: o.memset(hpad[:], 0.0), inc=1)
            a0 = 0 if i == 0 else 127
            hw = p.emit("dv", lambda o, a0=a0: o.tensor_mul(out=hpad[:, a0:a0 + 4096], in0=hsb[:], in1=win[:]),
                        inc=1, waits=[hm, w3_, hv3])
            hst = p.dma("gp", lambda o, i=i: o.dma_start(out=APX(hp_hbm, i * 128 * 4352, [[4352, 128], [1, 4352]]),
                                                         in_=hpad[:]), waits=[hw])
            hp_done.append(hst)
            p.barrier()
        bar = p.barrier()
        gld = [bar, bar]
        vld = [bar, bar]
        g1ld = [bar, bar]
        g2ld = [bar, bar]
        v2st = [bar, bar]
        v1_free = bar
        py_ev = [bar, bar]
        for c in range(128):
            par = c % 2
            lg1 = p.dma("sy", lambda o, c=c, par=par: o.dma_start(
                out=G[par][:, 0:4096],
                in_=APX(hp_hbm, 0 * 128 * 4352 + c * 4352, [[1, 128], [128, 32], [1, 128]])),
                waits=[gld[par]])
            lv = p.dma("gp", lambda o, c=c, par=par: o.dma_start(
                out=Vt[par][:], in_=APX(vq, c * 128 * 256, [[256, 128], [1, 256]])),
                waits=[vld[par]])
            lt1 = p.dma("gp", lambda o, c=c, par=par: o.dma_start(
                out=gt[par][:], in_=APX(gt1, c * 128 * 256, [[256, 128], [1, 256]])),
                waits=[g2ld[par]])
            mm = None
            for dd in range(32):
                ncols = 8 * (32 - dd)
                mm = p.emit("pe", lambda o, dd=dd, par=par, ncols=ncols: o.matmul(
                    out=pb[par][0:128, 8 * dd:256], lhsT=G[par][:, (31 - dd) * 128:(32 - dd) * 128],
                    rhs=Vt[par][:, 0:ncols], start=(dd == 0), stop=(dd == 31)),
                    inc=(1 if dd == 31 else 0),
                    waits=[lg1, lv, py_ev[par]] if dd == 0 else [])
            gld[par] = mm
            vld[par] = mm
            ev1 = p.emit("dv", lambda o, par=par: o.tensor_mul(out=v1t[:], in0=pb[par][0:128, 0:256],
                                                               in1=gt[par][:]), inc=1,
                         waits=[mm, lt1, v1_free])
            g1ld[par] = ev1
            lg2 = p.dma("sy", lambda o, c=c, par=par: o.dma_start(
                out=G[par][:, 0:4096],
                in_=APX(hp_hbm, 1 * 128 * 4352 + c * 4352, [[1, 128], [128, 32], [1, 128]])),
                waits=[mm])
            lt2 = p.dma("gp", lambda o, c=c, par=par: o.dma_start(
                out=gt[par][:], in_=APX(gt2, c * 128 * 256, [[256, 128], [1, 256]])),
                waits=[ev1])
            mm2_ = None
            for dd in range(32):
                ncols = 8 * (32 - dd)
                mm2_ = p.emit("pe", lambda o, dd=dd, par=par, ncols=ncols: o.matmul(
                    out=pb[2 + par][0:128, 8 * dd:256], lhsT=G[par][:, dd * 128:(dd + 1) * 128],
                    rhs=v1t[:, 0:ncols], start=(dd == 0), stop=(dd == 31)),
                    inc=(1 if dd == 31 else 0),
                    waits=[lg2, ev1, py_ev[par]] if dd == 0 else [])
            gld[par] = mm2_
            v1_free = mm2_
            ev2 = p.emit("dv", lambda o, par=par: o.tensor_mul(out=v2t[par][:], in0=pb[2 + par][0:128, 0:256],
                                                               in1=gt[par][:]), inc=1,
                         waits=[mm2_, lt2, v2st[par]])
            py_ev[par] = ev2
            g2ld[par] = ev2
            st = p.dma("gp", lambda o, c=c, par=par: o.dma_start(
                out=APX(v2_out, c * 128 * 256, [[256, 128], [1, 256]]), in_=v2t[par][:]),
                waits=[ev2])
            v2st[par] = st
        p.barrier()
        p.finalize(blk)
    return nc


def a2_inputs(core, v0_all, g1_all, g2_all, W):
    c0 = core * 128
    t = np.linspace(0, 1, L, dtype=np.float64)
    tr = t[::-1].copy()
    vq = np.ascontiguousarray(
        v0_all[:, c0:c0 + 128, :].reshape(8, 128, 32, 128).transpose(1, 3, 2, 0))
    g1s = g1_all[:, c0:c0 + 128, :].reshape(8, 128, 32, 128)[:, :, :, ::-1]
    g1q = np.ascontiguousarray(g1s.transpose(1, 3, 2, 0))
    g2q = np.ascontiguousarray(
        g2_all[:, c0:c0 + 128, :].reshape(8, 128, 32, 128).transpose(1, 3, 2, 0))
    return {
        "vq": vq.reshape(-1),
        "gt1": g1q.reshape(-1).astype(np.float32),
        "gt2": g2q.reshape(-1).astype(np.float32),
        "t7r": np.broadcast_to(tr.astype(np.float32), (7, L)).copy().reshape(-1),
        "t7n": np.broadcast_to(t.astype(np.float32), (7, L)).copy().reshape(-1),
        "tgr": np.broadcast_to((tr * L).astype(np.float32), (128, L)).copy().reshape(-1),
        "tgn": np.broadcast_to((t * L).astype(np.float32), (128, L)).copy().reshape(-1),
        "freqs": np.stack([np.concatenate([W["L0_attn_freqs"][i], W["L0_attn_freqs"][i], [0.0]]) for i in range(2)]).astype(np.float32).reshape(-1),
        "shvin": np.array([.5, .5, .5, .75, .75, .75, .5], np.float32),
        "mw1": W["L0_attn_mlp_w1"].reshape(-1),
        "mb1": W["L0_attn_mlp_b1"].reshape(-1),
        "mw2": W["L0_attn_mlp_w2"].reshape(-1),
        "mb2": W["L0_attn_mlp_b2"].reshape(-1),
        "mw3": np.ascontiguousarray(W["L0_attn_mlp_w3"][:, :, c0:c0 + 128]).reshape(-1),
        "mb3": np.ascontiguousarray(W["L0_attn_mlp_b3"][:, c0:c0 + 128]).reshape(-1),
        "dec": np.ascontiguousarray(W["L0_attn_decay"][:, c0:c0 + 128]).reshape(-1),
    }



def build_PROJ(with_ffn):
    nc = bass.Bass("TRN2", target_bir_lowering=False)
    es = contextlib.ExitStack()
    x_in = nc.dram_tensor("x", [L * D], F32, kind="ExternalInput")
    vin = nc.dram_tensor("vin", [D * L], BF16, kind="ExternalInput")
    wpf = nc.dram_tensor("wpf", [D * D], F32, kind="ExternalInput")
    bp = nc.dram_tensor("bp", [D], F32, kind="ExternalInput")
    wgTf = nc.dram_tensor("wgTf", [D * 8], F32, kind="ExternalInput")
    identf = nc.dram_tensor("identf", [128 * 128], F32, kind="ExternalInput")
    w1f = nc.dram_tensor("w1f", [EXP * D * HID], F32, kind="ExternalInput")
    w2f = nc.dram_tensor("w2f", [EXP * HID * D], F32, kind="ExternalInput")
    gwTf = nc.dram_tensor("gwTf", [D * 8], F32, kind="ExternalInput")
    b1r = nc.dram_tensor("b1r", [128 * 128], F32, kind="ExternalInput")
    b2 = nc.dram_tensor("b2", [8 * D], F32, kind="ExternalInput")
    gb0 = nc.dram_tensor("gb0", [8], F32, kind="ExternalInput")
    rmsw = nc.dram_tensor("rmsw", [8], F32, kind="ExternalInput")
    xo_out = nc.dram_tensor("xo", [L * D], F32, kind="ExternalOutput")
    z_out = nc.dram_tensor("zcm", [D * L], BF16, kind="ExternalOutput")
    sc_out = nc.dram_tensor("sc", [L * 8], F32, kind="ExternalOutput")
    w1bf = nc.dram_tensor("w1bf", [EXP * D * HID], BF16, kind="Internal")
    w2bf = nc.dram_tensor("w2bf", [EXP * HID * D], BF16, kind="Internal")
    gwbf = nc.dram_tensor("gwbf", [D * 8], BF16, kind="Internal")
    wpbf = nc.dram_tensor("wpbf", [D * D], BF16, kind="Internal")
    wgbf = nc.dram_tensor("wgbf", [D * 8], BF16, kind="Internal")
    xm_scr = nc.dram_tensor("xm_scr", [L * D], F32, kind="Internal")

    with es:
        V = {}
        p = P(nc, es)
        pb = [ps(nc, es, "pb%d" % i, (128, 512), F32) for i in range(7)]
        V["pb"] = pb
        V["pt_bf"] = es.enter_context(nc.psum_tensor("ptb", [128, 128], BF16))
        w1e_sb = sb(nc, es, "w1e", (128, 16384), BF16)
        w2e_sb = sb(nc, es, "w2e", (128, 16384), BF16)
        poolH = sb(nc, es, "poolH", (128, 16384), BF16)
        poolS = sb(nc, es, "poolS", (128, 8320), F32)
        poolZ = sb(nc, es, "poolZ", (128, 8192), BF16)
        zt = sb(nc, es, "zt", (128, 1024), BF16)
        V["w1e_sb"], V["w2e_sb"] = w1e_sb, w2e_sb
        V["h_sb"] = [poolH[:, i * 1024:(i + 1) * 1024] for i in range(16)]
        V["y_sb"] = [poolS[:, i * 1024:(i + 1) * 1024] for i in range(8)]
        V["z_cm"] = [poolZ[:, i * 1024:(i + 1) * 1024] for i in range(8)]
        V["castf"] = [poolS[:, 0:2048], poolS[:, 2048:4096]]
        V["castb"] = [poolZ[:, 0:2048], poolZ[:, 2048:4096]]
        V["lnst"] = sb(nc, es, "lnst", (128, 8), F32)
        V["lnsq"] = sb(nc, es, "lnsq", (128, 1024), F32)
        V["lnsq_free"] = T0
        V["lg"] = sb(nc, es, "lg", (128, 8), F32)
        V["lgs"] = sb(nc, es, "lgs", (128, 8), F32)
        V["gst"] = sb(nc, es, "gst", (128, 8), F32)
        V["ng"] = sb(nc, es, "ng", (128, 8), F32)
        V["gb_bc"] = sb(nc, es, "gbbc", (128, 8), F32)
        V["rmsw_bc"] = sb(nc, es, "rmswbc", (128, 8), F32)
        V["b2s_bc"] = sb(nc, es, "b2sbc", (128, 1024), F32)
        V["b1r_sb"] = sb(nc, es, "b1rsb", (128, 128), F32)
        V["ident_bf"] = sb(nc, es, "identb", (128, 128), BF16)
        V["ones1_f"] = sb(nc, es, "ones1", (1, 128), F32)
        V["row1"] = sb(nc, es, "row1", (1, 1040), F32)
        V["c_lneps"] = sb(nc, es, "clneps", (128, 2), F32)
        gatewT_sb = sb(nc, es, "gatewT", (128, 64), BF16)
        V["gatewT_sb"] = [gatewT_sb[:, i * 8:(i + 1) * 8] for i in range(8)]
        wg_sb = sb(nc, es, "wgsb", (128, 64), BF16)
        wp_sb = sb(nc, es, "wpsb", (128, 8192), BF16)
        bp_sb = sb(nc, es, "bpsb", (128, 1024), F32)
        xrow = sb(nc, es, "xrow", (128, 2048), F32)
        vt_sb = sb(nc, es, "vtsb", (128, 2048), BF16)
        scf = sb(nc, es, "scf", (128, 16), F32)

        blk = es.enter_context(nc.Block())
        p.emit("dv", lambda o: o.memset(V["ones1_f"][:], 1.0), inc=1)
        V["ones_tok"] = p.emit("dv", lambda o: o.memset(V["c_lneps"][:, 0:1], 1e-5), inc=1)
        p.dma("sy", lambda o: o.dma_start(out=V["b1r_sb"][:], in_=APX(b1r, 0, [[128, 128], [1, 128]])))
        idl = p.dma("sy", lambda o: o.dma_start(out=V["lnsq"][:, 0:128],
                                                in_=APX(identf, 0, [[128, 128], [1, 128]])))
        p.emit("dv", lambda o: o.tensor_copy(out=V["ident_bf"][:], in_=V["lnsq"][:, 0:128]),
               inc=1, waits=[idl])
        build_bcast_row(p, V, gb0, 0, 8, 0, V["gb_bc"])
        build_bcast_row(p, V, rmsw, 0, 8, 8, V["rmsw_bc"])
        lbp = p.dma("sy", lambda o: o.dma_start(out=V["row1"][0:1, 16:1040], in_=APX(bp, 0, [[D, 1], [1, D]])))
        mb1_ = p.emit("pe", lambda o: o.matmul(out=pb[0][0:128, 0:512], lhsT=V["ones1_f"][0:1, 0:128],
                                               rhs=V["row1"][0:1, 16:528], start=True, stop=True),
                      waits=[lbp, V["ones_tok"]])
        mb2_ = p.emit("pe", lambda o: o.matmul(out=pb[1][0:128, 0:512], lhsT=V["ones1_f"][0:1, 0:128],
                                               rhs=V["row1"][0:1, 528:1040], start=True, stop=True),
                      inc=1, waits=[])
        p.emit("dv", lambda o: o.tensor_copy(out=bp_sb[:, 0:512], in_=pb[0][0:128, 0:512]),
               inc=1, waits=[mb2_])
        bpt = p.emit("dv", lambda o: o.tensor_copy(out=bp_sb[:, 512:1024], in_=pb[1][0:128, 0:512]),
                     inc=1, waits=[mb2_])
        rowb = poolS
        l1 = p.dma("sy", lambda o: o.dma_start(out=rowb[0:1, 0:8192], in_=APX(b2, 0, [[8192, 1], [1, 8192]])))
        a1 = p.emit("dv", lambda o: o.tensor_add(out=rowb[0:1, 0:4096], in0=rowb[0:1, 0:4096],
                                                 in1=rowb[0:1, 4096:8192]), inc=1, waits=[l1])
        a2 = p.emit("dv", lambda o: o.tensor_add(out=rowb[0:1, 0:2048], in0=rowb[0:1, 0:2048],
                                                 in1=rowb[0:1, 2048:4096]), inc=1, waits=[a1])
        a3 = p.emit("dv", lambda o: o.tensor_add(out=rowb[0:1, 0:1024], in0=rowb[0:1, 0:1024],
                                                 in1=rowb[0:1, 1024:2048]), inc=1, waits=[a2])
        m3 = p.emit("pe", lambda o: o.matmul(out=pb[0][0:128, 0:512], lhsT=V["ones1_f"][0:1, 0:128],
                                             rhs=rowb[0:1, 0:512], start=True, stop=True),
                    waits=[a3, V["ones_tok"], bpt])
        m4 = p.emit("pe", lambda o: o.matmul(out=pb[1][0:128, 0:512], lhsT=V["ones1_f"][0:1, 0:128],
                                             rhs=rowb[0:1, 512:1024], start=True, stop=True),
                    inc=1, waits=[])
        p.emit("dv", lambda o: o.tensor_copy(out=V["b2s_bc"][:, 0:512], in_=pb[0][0:128, 0:512]),
               inc=1, waits=[m4])
        b2b = p.emit("dv", lambda o: o.tensor_copy(out=V["b2s_bc"][:, 512:1024], in_=pb[1][0:128, 0:512]),
                     inc=1, waits=[m4])
        V["b2s_tok"] = b2b
        V["w1bf_hbm"], V["w2bf_hbm"] = w1bf, w2bf
        casts = [(wpf, wpbf, D * D), (wgTf, wgbf, D * 8)]
        if with_ffn:
            casts += [(w1f, w1bf, EXP * D * HID), (w2f, w2bf, EXP * HID * D), (gwTf, gwbf, D * 8)]
        emit_cast_phase(p, V, casts, start=m4)
        if with_ffn:
            p.dma("sy", lambda o: o.dma_start(out=gatewT_sb[:, 0:64],
                                              in_=APX(gwbf, 0, [[8, 128], [8 * 128, 8], [1, 8]])))
        p.dma("sy", lambda o: o.dma_start(out=wg_sb[:, 0:64],
                                          in_=APX(wgbf, 0, [[8, 128], [8 * 128, 8], [1, 8]])))
        p.dma("sy", lambda o: o.dma_start(out=wp_sb[:, 0:8192],
                                          in_=APX(wpbf, 0, [[D, 128], [D * 128, 8], [1, D]])))
        bar = p.barrier()
        scev = bar

        for bc in range(4):
            tok0 = bc * 1024
            xld_free = [[bar], [bar]]
            vt_free = [bar, bar]
            pev = [bar, bar]
            dep_z = bar
            for s in range(8):
                par = s % 2
                xap = xrow[:, par * 1024:(par + 1) * 1024]
                lx = p.dma("gp", lambda o, s=s, par=par, tok0=tok0: o.dma_start(
                    out=xrow[:, par * 1024:(par + 1) * 1024],
                    in_=APX(x_in, (tok0 + s * 128) * D, [[D, 128], [1, D]])),
                    waits=xld_free[par])
                lv = p.dma("sy", lambda o, s=s, par=par, tok0=tok0: o.dma_start(
                    out=vt_sb[:, par * 1024:(par + 1) * 1024],
                    in_=APX(vin, tok0 + s * 128, [[L, 128], [L * 128, 8], [1, 128]])),
                    waits=[vt_free[par]])
                u2s = []
                for dc in range(2):
                    mm = None
                    for ct in range(8):
                        w = [lv, pev[dc], bar] if ct == 0 else []
                        mm = p.emit("pe", lambda o, dc=dc, ct=ct, par=par: o.matmul(
                            out=pb[2 + dc][0:128, 0:512],
                            lhsT=vt_sb[:, par * 1024 + ct * 128: par * 1024 + ct * 128 + 128],
                            rhs=wp_sb[:, ct * 1024 + dc * 512: ct * 1024 + dc * 512 + 512],
                            start=(ct == 0), stop=(ct == 7)),
                            inc=(1 if ct == 7 else 0), waits=w)
                    vt_free[par] = mm
                    u1 = p.emit("dv", lambda o, dc=dc, par=par: o.scalar_tensor_tensor(
                        out=xrow[:, par * 1024 + dc * 512: par * 1024 + dc * 512 + 512],
                        in0=pb[2 + dc][0:128, 0:512], scalar=1.0,
                        in1=xrow[:, par * 1024 + dc * 512: par * 1024 + dc * 512 + 512],
                        op0=ALU.mult, op1=ALU.add), inc=1, waits=[mm, lx])
                    u2 = p.emit("dv", lambda o, dc=dc, par=par: o.tensor_add(
                        out=xrow[:, par * 1024 + dc * 512: par * 1024 + dc * 512 + 512],
                        in0=xrow[:, par * 1024 + dc * 512: par * 1024 + dc * 512 + 512],
                        in1=bp_sb[:, dc * 512:(dc + 1) * 512]), inc=1, waits=[u1])
                    pev[dc] = u2
                    u2s.append(u2)
                xdst = xm_scr if with_ffn else xo_out
                st = p.dma("gp", lambda o, s=s, par=par, tok0=tok0, xdst=xdst: o.dma_start(
                    out=APX(xdst, (tok0 + s * 128) * D, [[D, 128], [1, D]]),
                    in_=xrow[:, par * 1024:(par + 1) * 1024]), waits=u2s)
                lnt = emit_ln(p, V, xap, zt[:, 0:1024], u2s)
                trt = emit_transpose(p, V, zt, V["z_cm"], s * 128, lnt)
                xld_free[par] = [st, trt]
                dep_z = trt
            if with_ffn:
                dep_g, y_use = emit_l0ffn(p, V, dep_z)
                dep_z2 = dep_z
                for s in range(8):
                    par = s % 2
                    xap = xrow[:, par * 1024:(par + 1) * 1024]
                    ld = p.dma("sy", lambda o, s=s, par=par, tok0=tok0: o.dma_start(
                        out=xrow[:, par * 1024:(par + 1) * 1024],
                        in_=APX(xm_scr, (tok0 + s * 128) * D, [[D, 128], [1, D]])),
                        waits=xld_free[par])
                    cmb = emit_l0_combine(p, V, s, xap, xap, y_use[s], dep_g, ld)
                    st = p.dma("gp", lambda o, s=s, par=par, tok0=tok0: o.dma_start(
                        out=APX(xo_out, (tok0 + s * 128) * D, [[D, 128], [1, D]]),
                        in_=xrow[:, par * 1024:(par + 1) * 1024]), waits=[cmb])
                    lnt = emit_ln(p, V, xap, zt[:, 0:1024], [cmb])
                    trt = emit_transpose(p, V, zt, V["z_cm"], s * 128, lnt)
                    xld_free[par] = [st, trt]
                    dep_z2 = trt
                dep_z = dep_z2
            zsts = []
            for ct in range(8):
                zst = p.dma("gp", lambda o, ct=ct, tok0=tok0: o.dma_start(
                    out=APX(z_out, ct * 128 * L + tok0, [[L, 128], [1, 1024]]),
                    in_=V["z_cm"][ct][:, 0:1024]), waits=[dep_z])
                zsts.append(zst)
            for s in range(8):
                par = s % 2
                mm = None
                for ct in range(8):
                    w = [dep_z, scev] if ct == 0 else []
                    mm = p.emit("pe", lambda o, s=s, ct=ct: o.matmul(
                        out=pb[4][0:128, 0:8], lhsT=V["z_cm"][ct][:, s * 128:(s + 1) * 128],
                        rhs=wg_sb[:, ct * 8:(ct + 1) * 8], start=(ct == 0), stop=(ct == 7)),
                        inc=(1 if ct == 7 else 0), waits=w)
                sev = p.emit("dv", lambda o, par=par: o.tensor_copy(out=scf[:, par * 8:(par + 1) * 8],
                                                                    in_=pb[4][0:128, 0:8]),
                             inc=1, waits=[mm, p.tok("dgp")])
                scev = sev
                p.dma("gp", lambda o, s=s, par=par, tok0=tok0: o.dma_start(
                    out=APX(sc_out, (tok0 + s * 128) * 8, [[8, 128], [1, 8]]),
                    in_=scf[:, par * 8:(par + 1) * 8]), waits=[sev])
            bar = p.barrier()
            V["w1_free"] = bar
            V["w2_free"] = bar
            V["h_free"] = bar
            V["y_init"] = bar
        p.finalize(blk)
    return nc



def build_L1FFN():
    nc = bass.Bass("TRN2", target_bir_lowering=False)
    es = contextlib.ExitStack()
    x_in = nc.dram_tensor("x", [L * D], F32, kind="ExternalInput")
    z_in = nc.dram_tensor("zcm", [D * L], BF16, kind="ExternalInput")
    grid_in = nc.dram_tensor("grid", [128 * 2048], F32, kind="ExternalInput")
    stm_in = nc.dram_tensor("stm", [L * 8], F32, kind="ExternalInput")
    seg_in = nc.dram_tensor("seg", [8 * 128], F32, kind="ExternalInput")
    seg2_in = nc.dram_tensor("seg2", [128 * 8], F32, kind="ExternalInput")
    identf = nc.dram_tensor("identf", [128 * 128], F32, kind="ExternalInput")
    w1Tf = nc.dram_tensor("w1Tf", [EXP * D * HID], F32, kind="ExternalInput")
    w3Tf = nc.dram_tensor("w3Tf", [EXP * D * HID], F32, kind="ExternalInput")
    w2Tf = nc.dram_tensor("w2Tf", [EXP * HID * D], F32, kind="ExternalInput")
    xo_out = nc.dram_tensor("xo", [L * D], F32, kind="ExternalOutput")
    w1bf = nc.dram_tensor("w1bf", [EXP * D * HID], BF16, kind="Internal")
    w3bf = nc.dram_tensor("w3bf", [EXP * D * HID], BF16, kind="Internal")
    w2bf = nc.dram_tensor("w2bf", [EXP * HID * D], BF16, kind="Internal")

    with es:
        V = {}
        p = P(nc, es)
        pb = [ps(nc, es, "pb%d" % i, (128, 512), F32) for i in range(7)]
        V["pb"] = pb
        ptf = es.enter_context(nc.psum_tensor("ptf", [128, 128], F32))
        w1sb = sb(nc, es, "w1e", (128, 16384), BF16)
        w3sb = sb(nc, es, "w3e", (128, 16384), BF16)
        w2sb = sb(nc, es, "w2e", (128, 16384), BF16)
        poolH = sb(nc, es, "poolH", (128, 16384), BF16)
        poolS = sb(nc, es, "poolS", (128, 8320), F32)
        zbuf = sb(nc, es, "zbuf", (128, 8192), BF16)
        grid = sb(nc, es, "gridsb", (128, 2048), F32)
        gw = sb(nc, es, "gw", (128, 2048), F32)
        h1t = sb(nc, es, "h1t", (128, 1024), BF16)
        wsb = sb(nc, es, "wsb", (128, 256), F32)
        idsb = sb(nc, es, "idsb", (128, 128), F32)
        segsb = sb(nc, es, "segsb", (8, 128), F32)
        seg2sb = sb(nc, es, "seg2sb", (128, 8), F32)
        st8 = sb(nc, es, "st8", (8, 8), F32)
        bc1 = sb(nc, es, "bc1", (128, 8), F32)
        fr8 = sb(nc, es, "fr8", (1, 24), F32)
        thf = sb(nc, es, "thf", (128, 16), F32)
        xrow = sb(nc, es, "xrow", (128, 2048), F32)
        V["y_sb"] = [poolS[:, i * 1024:(i + 1) * 1024] for i in range(8)]
        V["castf"] = [poolS[:, 0:2048], poolS[:, 2048:4096]]
        V["castb"] = [zbuf[:, 0:2048], zbuf[:, 2048:4096]]
        ones1 = sb(nc, es, "ones1", (1, 128), F32)

        blk = es.enter_context(nc.Block())
        o1 = p.emit("dv", lambda o: o.memset(ones1[:], 1.0), inc=1)
        lid = p.dma("sy", lambda o: o.dma_start(out=idsb[:], in_=APX(identf, 0, [[128, 128], [1, 128]])))
        lsg = p.dma("sy", lambda o: o.dma_start(out=segsb[:], in_=APX(seg_in, 0, [[128, 8], [1, 128]])))
        lsg2 = p.dma("sy", lambda o: o.dma_start(out=seg2sb[:], in_=APX(seg2_in, 0, [[8, 128], [1, 8]])))
        lgr = p.dma("sy", lambda o: o.dma_start(out=grid[:], in_=APX(grid_in, 0, [[2048, 128], [1, 2048]])))
        emit_cast_phase(p, V, [(w1Tf, w1bf, EXP * D * HID), (w3Tf, w3bf, EXP * D * HID),
                               (w2Tf, w2bf, EXP * HID * D)], start=T0)
        bar = p.barrier()
        r1 = p.emit("dv", lambda o: o.reduce_max(out=bc1[:, 2:3], in_=grid[:], axis=AX.X), inc=1, waits=[bar])
        r2 = p.emit("dv", lambda o: o.tensor_reduce(out=bc1[:, 3:4], in_=grid[:], axis=AX.X, op=ALU.min),
                    inc=1, waits=[bar])
        tr1 = p.emit("pe", lambda o: o.transpose(out=ptf[0:1, 0:128], in_=bc1[:, 2:3], identity=idsb[:]),
                     inc=1, waits=[r1, r2, bar])
        e1 = p.emit("dv", lambda o: o.tensor_copy(out=gw[0:1, 0:128], in_=ptf[0:1, 0:128]), inc=1, waits=[tr1])
        tr1b = p.emit("pe", lambda o: o.transpose(out=ptf[0:1, 0:128], in_=bc1[:, 3:4], identity=idsb[:]),
                      inc=1, waits=[e1])
        e1b = p.emit("dv", lambda o: o.tensor_copy(out=gw[0:1, 128:256], in_=ptf[0:1, 0:128]), inc=1, waits=[tr1b])
        gmx = p.emit("dv", lambda o: o.reduce_max(out=fr8[0:1, 0:1], in_=gw[0:1, 0:128], axis=AX.X),
                     inc=1, waits=[e1, e1b])
        gmn = p.emit("dv", lambda o: o.tensor_reduce(out=fr8[0:1, 1:2], in_=gw[0:1, 128:256], axis=AX.X, op=ALU.min),
                     inc=1, waits=[e1b])
        mmlo = p.emit("pe", lambda o: o.matmul(out=pb[5][0:8, 0:1], lhsT=ones1[0:1, 0:8],
                                               rhs=fr8[0:1, 1:2], start=True, stop=True), waits=[gmn, o1])
        mmhi = p.emit("pe", lambda o: o.matmul(out=pb[6][0:8, 0:1], lhsT=ones1[0:1, 0:8],
                                               rhs=fr8[0:1, 0:1], start=True, stop=True), inc=1, waits=[gmx])
        elo = p.emit("dv", lambda o: o.tensor_copy(out=st8[:, 0:1], in_=pb[5][0:8, 0:1]), inc=1, waits=[mmhi])
        ehi = p.emit("dv", lambda o: o.tensor_copy(out=st8[:, 1:2], in_=pb[6][0:8, 0:1]), inc=1, waits=[mmhi])
        mid0 = p.emit("dv", lambda o: o.tensor_add(out=st8[:, 2:3], in0=st8[:, 0:1], in1=st8[:, 1:2]),
                      inc=1, waits=[elo, ehi])
        prev = p.emit("dv", lambda o: o.tensor_single_scalar(out=st8[:, 2:3], in_=st8[:, 2:3],
                                                             scalar=0.5, op=ALU.mult), inc=1, waits=[mid0])
        mmM = p.emit("pe", lambda o: o.matmul(out=pb[5][0:128, 0:1], lhsT=ones1[0:1, 0:128],
                                              rhs=fr8[0:1, 0:1], start=True, stop=True), inc=1,
                     waits=[gmx, elo])
        eM = p.emit("dv", lambda o: o.tensor_single_scalar(out=bc1[:, 1:2], in_=pb[5][0:128, 0:1],
                                                           scalar=-1.0, op=ALU.mult), inc=1, waits=[mmM])
        for it in range(46):
            mmb = p.emit("pe", lambda o: o.matmul(out=pb[4][0:128, 0:1], lhsT=segsb[:],
                                                  rhs=st8[:, 2:3], start=True, stop=True),
                         inc=1, waits=[prev, lsg, eM])
            eb = p.emit("dv", lambda o: o.tensor_copy(out=bc1[:, 0:1], in_=pb[4][0:128, 0:1]),
                        inc=1, waits=[mmb])
            ge = p.emit("dv", lambda o: o.tensor_single_scalar(out=gw[:], in_=grid[:],
                                                               scalar=bc1[:, 0:1], op=ALU.is_ge),
                        inc=1, waits=[eb])
            rs = p.emit("dv", lambda o: o.reduce_sum(out=bc1[:, 4:5], in_=gw[:], axis=AX.X),
                        inc=1, waits=[ge])
            mmc = p.emit("pe", lambda o: o.matmul(out=pb[4][0:8, 64:65], lhsT=seg2sb[:],
                                                  rhs=bc1[:, 4:5], start=True, stop=True),
                         inc=1, waits=[rs, lsg2])
            ec = p.emit("dv", lambda o: o.tensor_copy(out=st8[:, 3:4], in_=pb[4][0:8, 64:65]),
                        inc=1, waits=[mmc])
            g1 = p.emit("dv", lambda o: o.tensor_single_scalar(out=st8[:, 4:5], in_=st8[:, 3:4],
                                                               scalar=4096.0, op=ALU.is_ge),
                        inc=1, waits=[ec])
            d1 = p.emit("dv", lambda o: o.tensor_sub(out=st8[:, 6:7], in0=st8[:, 2:3], in1=st8[:, 0:1]),
                        inc=1, waits=[g1])
            d2 = p.emit("dv", lambda o: o.tensor_mul(out=st8[:, 6:7], in0=st8[:, 6:7], in1=st8[:, 4:5]),
                        inc=1, waits=[d1])
            d3 = p.emit("dv", lambda o: o.tensor_add(out=st8[:, 0:1], in0=st8[:, 0:1], in1=st8[:, 6:7]),
                        inc=1, waits=[d2])
            g2 = p.emit("dv", lambda o: o.tensor_scalar(out=st8[:, 5:6], in0=st8[:, 4:5], scalar1=-1.0,
                                                        scalar2=1.0, op0=ALU.mult, op1=ALU.add),
                        inc=1, waits=[d3])
            d4 = p.emit("dv", lambda o: o.tensor_sub(out=st8[:, 6:7], in0=st8[:, 2:3], in1=st8[:, 1:2]),
                        inc=1, waits=[g2])
            d5 = p.emit("dv", lambda o: o.tensor_mul(out=st8[:, 6:7], in0=st8[:, 6:7], in1=st8[:, 5:6]),
                        inc=1, waits=[d4])
            d6 = p.emit("dv", lambda o: o.tensor_add(out=st8[:, 1:2], in0=st8[:, 1:2], in1=st8[:, 6:7]),
                        inc=1, waits=[d5])
            d7 = p.emit("dv", lambda o: o.tensor_add(out=st8[:, 2:3], in0=st8[:, 0:1], in1=st8[:, 1:2]),
                        inc=1, waits=[d6])
            prev = p.emit("dv", lambda o: o.tensor_single_scalar(out=st8[:, 2:3], in_=st8[:, 2:3],
                                                                 scalar=0.5, op=ALU.mult), inc=1, waits=[d7])
        mmth = p.emit("pe", lambda o: o.matmul(out=pb[4][0:128, 0:1], lhsT=segsb[:],
                                               rhs=st8[:, 0:1], start=True, stop=True), inc=1, waits=[prev])
        eth = p.emit("dv", lambda o: o.tensor_copy(out=bc1[:, 0:1], in_=pb[4][0:128, 0:1]), inc=1, waits=[mmth])
        ge2 = p.emit("dv", lambda o: o.tensor_single_scalar(out=gw[:], in_=grid[:], scalar=bc1[:, 0:1],
                                                            op=ALU.is_ge), inc=1, waits=[eth])
        ex1 = p.emit("ac", lambda o: o.activation(out=grid[:], in_=grid[:], func=AF.Exp, bias=bc1[:, 1:2]),
                     inc=1, waits=[ge2])
        me1 = p.emit("dv", lambda o: o.tensor_mul(out=gw[:], in0=gw[:], in1=grid[:]), inc=1, waits=[ex1])
        rs2 = p.emit("dv", lambda o: o.reduce_sum(out=bc1[:, 4:5], in_=gw[:], axis=AX.X), inc=1, waits=[me1])
        mmD = p.emit("pe", lambda o: o.matmul(out=pb[4][0:8, 64:65], lhsT=seg2sb[:], rhs=bc1[:, 4:5],
                                              start=True, stop=True), inc=1, waits=[rs2])
        eD = p.emit("dv", lambda o: o.tensor_copy(out=st8[:, 3:4], in_=pb[4][0:8, 64:65]), inc=1, waits=[mmD])
        rD = p.emit("dv", lambda o: o.reciprocal(out=st8[:, 4:5], in_=st8[:, 3:4]), inc=1, waits=[eD])
        tr2 = p.emit("pe", lambda o: o.transpose(out=ptf[0:1, 0:8], in_=st8[:, 0:1], identity=idsb[0:8, 0:8]),
                     inc=1, waits=[rD])
        er2 = p.emit("dv", lambda o: o.tensor_copy(out=fr8[0:1, 8:16], in_=ptf[0:1, 0:8]), inc=1, waits=[tr2])
        tr3 = p.emit("pe", lambda o: o.transpose(out=ptf[0:1, 0:8], in_=st8[:, 4:5], identity=idsb[0:8, 0:8]),
                     inc=1, waits=[er2])
        er3 = p.emit("dv", lambda o: o.tensor_copy(out=fr8[0:1, 16:24], in_=ptf[0:1, 0:8]), inc=1, waits=[tr3])
        mmt = p.emit("pe", lambda o: o.matmul(out=pb[4][0:128, 0:8], lhsT=ones1[:], rhs=fr8[0:1, 8:16],
                                              start=True, stop=True), waits=[er3])
        mmr = p.emit("pe", lambda o: o.matmul(out=pb[4][0:128, 8:16], lhsT=ones1[:], rhs=fr8[0:1, 16:24],
                                              start=True, stop=True), inc=1, waits=[])
        ethf = p.emit("dv", lambda o: o.tensor_copy(out=thf[:, 0:16], in_=pb[4][0:128, 0:16]), inc=1, waits=[mmr])
        wprev = ethf
        for tt in range(32):
            lsc = p.dma("sy", lambda o, tt=tt: o.dma_start(out=bc1[:, 2:10] if False else gw[:, 0:8],
                                                           in_=APX(stm_in, tt * 128 * 8, [[8, 128], [1, 8]])),
                        waits=[wprev])
            wx = p.emit("ac", lambda o, tt=tt: o.activation(out=wsb[:, tt * 8:(tt + 1) * 8], in_=gw[:, 0:8],
                                                            func=AF.Exp, bias=bc1[:, 1:2]), inc=1,
                        waits=[lsc, ethf])
            wg2 = p.emit("dv", lambda o, tt=tt: o.tensor_tensor(out=gw[:, 8:16], in0=gw[:, 0:8],
                                                                in1=thf[:, 0:8], op=ALU.is_ge), inc=1,
                         waits=[wx])
            wm1 = p.emit("dv", lambda o, tt=tt: o.tensor_mul(out=wsb[:, tt * 8:(tt + 1) * 8],
                                                             in0=wsb[:, tt * 8:(tt + 1) * 8],
                                                             in1=gw[:, 8:16]), inc=1, waits=[wg2])
            wprev = p.emit("dv", lambda o, tt=tt: o.tensor_mul(out=wsb[:, tt * 8:(tt + 1) * 8],
                                                               in0=wsb[:, tt * 8:(tt + 1) * 8],
                                                               in1=thf[:, 8:16]), inc=1, waits=[wm1])
        bar2 = p.barrier()
        w1fr, w2fr, w3fr = bar2, bar2, bar2
        hfree = bar2
        yinit = bar2
        for bc in range(4):
            tok0 = bc * 1024
            lz = p.dma("sy", lambda o, tok0=tok0: o.dma_start(
                out=zbuf[:, 0:8192],
                in_=APX(z_in, tok0, [[L, 128], [L * 128, 8], [1, 1024]])), waits=[yinit])
            z_cm = [zbuf[:, i * 1024:(i + 1) * 1024] for i in range(8)]
            y_use = [yinit] * 8
            y_written = [False] * 16
            yev = [bar2, bar2]
            for e in range(EXP):
                l1_ = p.dma("sy", lambda o, e=e: o.dma_start(
                    out=w1sb[:, 0:16384],
                    in_=APX(w1bf, e * D * HID, [[HID, 128], [HID * 128, 8], [1, HID]])), waits=[w1fr])
                l3_ = p.dma("sy", lambda o, e=e: o.dma_start(
                    out=w3sb[:, 0:16384],
                    in_=APX(w3bf, e * D * HID, [[HID, 128], [HID * 128, 8], [1, HID]])), waits=[w3fr])
                l2_ = p.dma("sy", lambda o, e=e: o.dma_start(
                    out=w2sb[:, 0:16384],
                    in_=APX(w2bf, e * HID * D, [[D, 128], [D * 128, 16], [1, D]])), waits=[w2fr])
                hdeps = []
                sev = [bar2, bar2]
                mmh = None
                for ht in range(16):
                    for sc in range(2):
                        for ct in range(8):
                            w = []
                            if ct == 0:
                                w = [sev[sc]]
                                if ht == 0 and sc == 0:
                                    w += [l1_, lz, hfree]
                            mmh = p.emit("pe", lambda o, ht=ht, sc=sc, ct=ct: o.matmul(
                                out=pb[sc][0:128, 0:512],
                                lhsT=w1sb[:, ct * HID + ht * 128: ct * HID + ht * 128 + 128],
                                rhs=z_cm[ct][:, sc * 512:(sc + 1) * 512],
                                start=(ct == 0), stop=(ct == 7)),
                                inc=(1 if ct == 7 else 0), waits=w)
                        sl = p.emit("ac", lambda o, ht=ht, sc=sc: o.activation(
                            out=h1t[:, sc * 512:(sc + 1) * 512], in_=pb[sc][0:128, 0:512],
                            func=AF.Silu), inc=1, waits=[mmh])
                        mmb_ = None
                        for ct in range(8):
                            w = [sl, l3_] if ct == 0 else []
                            mmb_ = p.emit("pe", lambda o, ht=ht, sc=sc, ct=ct: o.matmul(
                                out=pb[2 + sc][0:128, 0:512],
                                lhsT=w3sb[:, ct * HID + ht * 128: ct * HID + ht * 128 + 128],
                                rhs=z_cm[ct][:, sc * 512:(sc + 1) * 512],
                                start=(ct == 0), stop=(ct == 7)),
                                inc=(1 if ct == 7 else 0), waits=w)
                        hv = p.emit("dv", lambda o, ht=ht, sc=sc: o.tensor_mul(
                            out=V_hsb(poolH, ht)[:, sc * 512:(sc + 1) * 512],
                            in0=pb[2 + sc][0:128, 0:512],
                            in1=h1t[:, sc * 512:(sc + 1) * 512]), inc=1, waits=[mmb_])
                        sev[sc] = hv
                        hdeps.append(hv)
                dep_h = hdeps[-1]
                w1fr = mmh
                w3fr = mmh
                mmyl = None
                for s in range(8):
                    for dc in range(2):
                        idx = s * 2 + dc
                        par = idx % 2
                        mm = None
                        for ht in range(16):
                            w = []
                            if ht == 0:
                                w = [dep_h, l2_, yev[par]]
                                if dc == 0:
                                    w.append(y_use[s])
                            mm = p.emit("pe", lambda o, s=s, dc=dc, ht=ht, par=par: o.matmul(
                                out=pb[4 + par][0:128, 0:512],
                                lhsT=V_hsb(poolH, ht)[:, s * 128:(s + 1) * 128],
                                rhs=w2sb[:, ht * D + dc * 512: ht * D + dc * 512 + 512],
                                start=(ht == 0), stop=(ht == 15)),
                                inc=(1 if ht == 15 else 0), waits=w)
                        mmyl = mm
                        widx = (bc * 8 + s) * 8 + e
                        if not y_written[idx]:
                            tv = p.emit("dv", lambda o, s=s, dc=dc, par=par, widx=widx: o.tensor_scalar(
                                out=V["y_sb"][s][:, dc * 512:(dc + 1) * 512],
                                in0=pb[4 + par][0:128, 0:512],
                                scalar1=wsb[:, widx:widx + 1], scalar2=None,
                                op0=ALU.mult), inc=1, waits=[mm])
                            y_written[idx] = True
                        else:
                            tv = p.emit("dv", lambda o, s=s, dc=dc, par=par, widx=widx: o.scalar_tensor_tensor(
                                out=V["y_sb"][s][:, dc * 512:(dc + 1) * 512],
                                in0=pb[4 + par][0:128, 0:512], scalar=wsb[:, widx:widx + 1],
                                in1=V["y_sb"][s][:, dc * 512:(dc + 1) * 512],
                                op0=ALU.mult, op1=ALU.add), inc=1, waits=[mm])
                        yev[par] = tv
                    y_use[s] = tv
                hfree = mmyl
                w2fr = mmyl
            xfree = [yinit, yinit]
            outs = []
            for s in range(8):
                par = s % 2
                lx = p.dma("gp", lambda o, s=s, par=par, tok0=tok0: o.dma_start(
                    out=xrow[:, par * 1024:(par + 1) * 1024],
                    in_=APX(x_in, (tok0 + s * 128) * D, [[D, 128], [1, D]])), waits=[xfree[par]])
                cm = p.emit("dv", lambda o, s=s, par=par: o.scalar_tensor_tensor(
                    out=xrow[:, par * 1024:(par + 1) * 1024], in0=V["y_sb"][s][:, 0:1024],
                    scalar=0.5, in1=xrow[:, par * 1024:(par + 1) * 1024],
                    op0=ALU.mult, op1=ALU.add), inc=1, waits=[lx, y_use[s]])
                stx = p.dma("gp", lambda o, s=s, par=par, tok0=tok0: o.dma_start(
                    out=APX(xo_out, (tok0 + s * 128) * D, [[D, 128], [1, D]]),
                    in_=xrow[:, par * 1024:(par + 1) * 1024]), waits=[cm])
                xfree[par] = stx
                outs.append(stx)
            yinit = p.barrier()
            w1fr = w2fr = w3fr = hfree = yinit
        p.finalize(blk)
    return nc


def V_hsb(poolH, ht):
    return poolH[:, ht * 1024:(ht + 1) * 1024]



def build_QKV():
    nc = bass.Bass("TRN2", target_bir_lowering=False)
    es = contextlib.ExitStack()
    x_in = nc.dram_tensor("x", [L * D], F32, kind="ExternalInput")
    identf = nc.dram_tensor("identf", [128 * 128], F32, kind="ExternalInput")
    wdqf = nc.dram_tensor("wdqf", [D * 256], F32, kind="ExternalInput")
    wdkf = nc.dram_tensor("wdkf", [D * 256], F32, kind="ExternalInput")
    wuqf = nc.dram_tensor("wuqf", [256 * D], F32, kind="ExternalInput")
    wukf = nc.dram_tensor("wukf", [256 * D], F32, kind="ExternalInput")
    wuvf = nc.dram_tensor("wuvf", [256 * D], F32, kind="ExternalInput")
    q_out = nc.dram_tensor("q", [D * L], BF16, kind="ExternalOutput")
    k_out = nc.dram_tensor("k", [D * L], BF16, kind="ExternalOutput")
    v_out = nc.dram_tensor("v", [L * D], BF16, kind="ExternalOutput")
    wdqb = nc.dram_tensor("wdqb", [D * 256], BF16, kind="Internal")
    wdkb = nc.dram_tensor("wdkb", [D * 256], BF16, kind="Internal")
    wuqb = nc.dram_tensor("wuqb", [256 * D], BF16, kind="Internal")
    wukb = nc.dram_tensor("wukb", [256 * D], BF16, kind="Internal")
    wuvb = nc.dram_tensor("wuvb", [256 * D], BF16, kind="Internal")

    with es:
        V = {}
        p = P(nc, es)
        pb = [ps(nc, es, "pb%d" % i, (128, 512), F32) for i in range(7)]
        V["pb"] = pb
        V["pt_bf"] = es.enter_context(nc.psum_tensor("ptb", [128, 128], BF16))
        poolS = sb(nc, es, "poolS", (128, 4096), F32)
        zcm = sb(nc, es, "zcm", (128, 4096), BF16)
        zt = sb(nc, es, "zt", (128, 1024), BF16)
        xrow = sb(nc, es, "xrow", (128, 2048), F32)
        wdq_sb = sb(nc, es, "wdq", (128, 2048), BF16)
        wdk_sb = sb(nc, es, "wdk", (128, 2048), BF16)
        wuq_sb = sb(nc, es, "wuq", (128, 2048), BF16)
        wuk_sb = sb(nc, es, "wuk", (128, 2048), BF16)
        wuv_sb = sb(nc, es, "wuv", (128, 2048), BF16)
        cq_sb = sb(nc, es, "cq", (128, 1024), BF16)
        ck_sb = sb(nc, es, "ck", (128, 1024), BF16)
        ev_sb = sb(nc, es, "ev", (128, 2048), BF16)
        V["lnst"] = sb(nc, es, "lnst", (128, 8), F32)
        V["lnsq"] = sb(nc, es, "lnsq", (128, 1024), F32)
        V["lnsq_free"] = T0
        V["ident_bf"] = sb(nc, es, "identb", (128, 128), BF16)
        V["c_lneps"] = sb(nc, es, "clneps", (128, 2), F32)
        V["castf"] = [poolS[:, 0:2048], poolS[:, 2048:4096]]
        V["castb"] = [zcm[:, 0:2048], zcm[:, 2048:4096]]

        blk = es.enter_context(nc.Block())
        p.emit("dv", lambda o: o.memset(V["c_lneps"][:, 0:1], 1e-5), inc=1)
        idl = p.dma("sy", lambda o: o.dma_start(out=V["lnsq"][:, 0:128],
                                                in_=APX(identf, 0, [[128, 128], [1, 128]])))
        p.emit("dv", lambda o: o.tensor_copy(out=V["ident_bf"][:], in_=V["lnsq"][:, 0:128]),
               inc=1, waits=[idl])
        emit_cast_phase(p, V, [(wdqf, wdqb, D * 256), (wdkf, wdkb, D * 256),
                               (wuqf, wuqb, 256 * D), (wukf, wukb, 256 * D),
                               (wuvf, wuvb, 256 * D)], start=T0)
        p.dma("sy", lambda o: o.dma_start(out=wdq_sb[:], in_=APX(wdqb, 0, [[256, 128], [256 * 128, 8], [1, 256]])))
        p.dma("sy", lambda o: o.dma_start(out=wdk_sb[:], in_=APX(wdkb, 0, [[256, 128], [256 * 128, 8], [1, 256]])))
        p.dma("sy", lambda o: o.dma_start(out=wuq_sb[:], in_=APX(wuqb, 0, [[D, 128], [D * 128, 2], [1, D]])))
        p.dma("sy", lambda o: o.dma_start(out=wuk_sb[:], in_=APX(wukb, 0, [[D, 128], [D * 128, 2], [1, D]])))
        p.dma("sy", lambda o: o.dma_start(out=wuv_sb[:], in_=APX(wuvb, 0, [[D, 128], [D * 128, 2], [1, D]])))
        bar = p.barrier()

        for ch in range(8):
            tok0 = ch * 512
            xfree = [[bar], [bar]]
            depz = bar
            for s in range(4):
                par = s % 2
                xap = xrow[:, par * 1024:(par + 1) * 1024]
                lx = p.dma("gp", lambda o, s=s, par=par, tok0=tok0: o.dma_start(
                    out=xrow[:, par * 1024:(par + 1) * 1024],
                    in_=APX(x_in, (tok0 + s * 128) * D, [[D, 128], [1, D]])), waits=xfree[par])
                lnt = emit_ln(p, V, xap, zt[:, 0:1024], [lx])
                zcm_t = [zcm[:, i * 512:(i + 1) * 512] for i in range(8)]
                trt = emit_transpose(p, V, zt, zcm_t, s * 128, lnt)
                xfree[par] = [trt]
                depz = trt
            zcm_t = [zcm[:, i * 512:(i + 1) * 512] for i in range(8)]
            cqe = []
            for (wsb_, csb, pidx) in ((wdq_sb, cq_sb, 0), (wdk_sb, ck_sb, 1)):
                for dct in range(2):
                    mm = None
                    for ct in range(8):
                        w = [depz, V.get("cq_free%d" % pidx, bar)] if ct == 0 else []
                        mm = p.emit("pe", lambda o, dct=dct, ct=ct, wsb_=wsb_, pidx=pidx: o.matmul(
                            out=pb[pidx][0:128, 0:512],
                            lhsT=wsb_[:, ct * 256 + dct * 128: ct * 256 + dct * 128 + 128],
                            rhs=zcm_t[ct], start=(ct == 0), stop=(ct == 7)),
                            inc=(1 if ct == 7 else 0), waits=w)
                    cv = p.emit("dv", lambda o, dct=dct, csb=csb, pidx=pidx: o.tensor_copy(
                        out=csb[:, dct * 512:(dct + 1) * 512], in_=pb[pidx][0:128, 0:512]),
                        inc=1, waits=[mm])
                    V["cq_free%d" % pidx] = cv
                    cqe.append(cv)
            dep_c = cqe[-1]
            stq = []
            for (wsb_, csb, outh, pidx) in ((wuq_sb, cq_sb, q_out, 2), (wuk_sb, ck_sb, k_out, 3)):
                for dqt in range(8):
                    mm = None
                    for dct in range(2):
                        w = cqe + [V.get("qk_free%d" % pidx, bar)] if dct == 0 else []
                        mm = p.emit("pe", lambda o, dqt=dqt, dct=dct, wsb_=wsb_, csb=csb, pidx=pidx: o.matmul(
                            out=pb[pidx][0:128, 0:512],
                            lhsT=wsb_[:, dct * 1024 + dqt * 128: dct * 1024 + dqt * 128 + 128],
                            rhs=csb[:, dct * 512:(dct + 1) * 512][0:128, 0:512],
                            start=(dct == 0), stop=(dct == 1)),
                            inc=(1 if dct == 1 else 0), waits=w)
                    ev = p.emit("dv", lambda o, dqt=dqt, pidx=pidx: o.tensor_copy(
                        out=ev_sb[:, (pidx - 2) * 1024 + (dqt % 2) * 512: (pidx - 2) * 1024 + (dqt % 2) * 512 + 512],
                        in_=pb[pidx][0:128, 0:512]), inc=1, waits=[mm, V.get("evq_free%d" % ((pidx - 2) * 2 + dqt % 2), bar)])
                    st = p.dma("gp", lambda o, dqt=dqt, pidx=pidx, tok0=tok0: o.dma_start(
                        out=APX(outh, dqt * 128 * L + tok0, [[L, 128], [1, 512]]),
                        in_=ev_sb[:, (pidx - 2) * 1024 + (dqt % 2) * 512: (pidx - 2) * 1024 + (dqt % 2) * 512 + 512]),
                        waits=[ev])
                    V["evq_free%d" % ((pidx - 2) * 2 + dqt % 2)] = st
                    V["qk_free%d" % pidx] = ev
                    stq.append(st)
            for s in range(4):
                for dvc in range(2):
                    mm = None
                    for dct in range(2):
                        w = [dep_c, V.get("v_free", bar)] if dct == 0 else []
                        mm = p.emit("pe", lambda o, s=s, dvc=dvc, dct=dct: o.matmul(
                            out=pb[4][0:128, 0:512],
                            lhsT=ck_sb[:, dct * 512 + s * 128: dct * 512 + s * 128 + 128],
                            rhs=wuv_sb[:, dct * 1024 + dvc * 512: dct * 1024 + dvc * 512 + 512],
                            start=(dct == 0), stop=(dct == 1)),
                            inc=(1 if dct == 1 else 0), waits=w)
                    ev = p.emit("dv", lambda o, s=s, dvc=dvc: o.tensor_copy(
                        out=ev_sb[:, 0:512], in_=pb[4][0:128, 0:512]), inc=1,
                        waits=[mm, V.get("vev_free", bar)])
                    st = p.dma("gp", lambda o, s=s, dvc=dvc, tok0=tok0: o.dma_start(
                        out=APX(v_out, (tok0 + s * 128) * D + dvc * 512, [[D, 128], [1, 512]]),
                        in_=ev_sb[:, 0:512]), waits=[ev])
                    V["vev_free"] = st
                    V["v_free"] = ev
            bar = p.barrier()
        p.finalize(blk)
    return nc



def build_ATTN():
    nc = bass.Bass("TRN2", target_bir_lowering=False)
    es = contextlib.ExitStack()
    q_in = nc.dram_tensor("q", [D * L], BF16, kind="ExternalInput")
    k_in = nc.dram_tensor("k", [D * L], BF16, kind="ExternalInput")
    v_in = nc.dram_tensor("v", [L * D], BF16, kind="ExternalInput")
    identf = nc.dram_tensor("identf", [128 * 128], F32, kind="ExternalInput")
    maskb = nc.dram_tensor("maskb", [4 * 128 * 512], BF16, kind="ExternalInput")
    a_out = nc.dram_tensor("attn", [D * L], BF16, kind="ExternalOutput")

    with es:
        p = P(nc, es)
        ps_s = [es.enter_context(nc.psum_tensor("pss%d" % i, [128, 512], F32)) for i in range(2)]
        ps_l = es.enter_context(nc.psum_tensor("psl", [1, 512], F32))
        ps_av = es.enter_context(nc.psum_tensor("psav", [64, 512], F32))
        ps_b = es.enter_context(nc.psum_tensor("psb", [64, 512], F32))
        ptf = es.enter_context(nc.psum_tensor("ptf", [128, 128], F32))
        ksb = sb(nc, es, "ksb", (64, 4096), BF16)
        qsb = sb(nc, es, "qsb", (64, 4096), BF16)
        vsb = sb(nc, es, "vsb", (128, 2048), BF16)
        Pt = [sb(nc, es, "P%d" % i, (128, 512), BF16) for i in range(2)]
        msk = sb(nc, es, "msk", (128, 2048), BF16)
        idsb = sb(nc, es, "idsb", (128, 128), F32)
        ones1 = sb(nc, es, "ones1", (1, 128), F32)
        onesb = sb(nc, es, "onesb", (128, 1), BF16)
        nrm = sb(nc, es, "nrm", (128, 8), F32)
        row = sb(nc, es, "row", (1, 640), F32)
        ab = sb(nc, es, "ab", (64, 1024), F32)
        rl = sb(nc, es, "rl", (1, 512), F32)
        ao = sb(nc, es, "ao", (64, 512), BF16)

        blk = es.enter_context(nc.Block())
        o1 = p.emit("dv", lambda o: o.memset(ones1[:], 1.0), inc=1)
        o2 = p.emit("dv", lambda o: o.memset(onesb[:], 1.0), inc=1)
        lid = p.dma("sy", lambda o: o.dma_start(out=idsb[:], in_=APX(identf, 0, [[128, 128], [1, 128]])))
        lmk = p.dma("sy", lambda o: o.dma_start(out=msk[:], in_=APX(maskb, 0, [[512, 128], [512 * 128, 4], [1, 512]])))
        bar = p.barrier()
        kfree, qfree, vfree = bar, bar, bar
        aofree = bar
        abfree = bar
        for h in range(16):
            lk = p.dma("sy", lambda o, h=h: o.dma_start(out=ksb[:], in_=APX(k_in, h * 64 * L, [[L, 64], [1, L]])),
                       waits=[kfree])
            lq = p.dma("sy", lambda o, h=h: o.dma_start(out=qsb[:], in_=APX(q_in, h * 64 * L, [[L, 64], [1, L]])),
                       waits=[qfree])
            lv = p.dma("gp", lambda o, h=h: o.dma_start(out=vsb[:], in_=APX(v_in, h * 64, [[D, 128], [D * 128, 32], [1, 64]])),
                       waits=[vfree])
            n1 = p.emit("dv", lambda o: o.reduce_max(out=nrm[0:64, 0:1], in_=ksb[:], axis=AX.X,
                                                     apply_absolute_value=True), inc=1, waits=[lk, bar])
            n2 = p.emit("dv", lambda o: o.reduce_max(out=nrm[0:64, 1:2], in_=qsb[:], axis=AX.X,
                                                     apply_absolute_value=True), inc=1, waits=[lq])
            t1 = p.emit("pe", lambda o: o.transpose(out=ptf[0:1, 0:64], in_=nrm[0:64, 0:1], identity=idsb[0:64, 0:64]),
                        inc=1, waits=[n1, n2])
            t2 = p.emit("dv", lambda o: o.tensor_copy(out=row[0:1, 0:64], in_=ptf[0:1, 0:64]), inc=1, waits=[t1])
            mK = p.emit("dv", lambda o: o.reduce_max(out=row[0:1, 256:257], in_=row[0:1, 0:64], axis=AX.X),
                        inc=1, waits=[t2])
            t3 = p.emit("pe", lambda o: o.transpose(out=ptf[0:1, 0:64], in_=nrm[0:64, 1:2], identity=idsb[0:64, 0:64]),
                        inc=1, waits=[t2])
            t4 = p.emit("dv", lambda o: o.tensor_copy(out=row[0:1, 64:128], in_=ptf[0:1, 0:64]), inc=1, waits=[t3])
            mQ = p.emit("dv", lambda o: o.reduce_max(out=row[0:1, 257:258], in_=row[0:1, 64:128], axis=AX.X),
                        inc=1, waits=[t4])
            c1 = p.emit("dv", lambda o: o.tensor_mul(out=row[0:1, 258:259], in0=row[0:1, 256:257],
                                                     in1=row[0:1, 257:258]), inc=1, waits=[mK, mQ])
            c2 = p.emit("dv", lambda o: o.tensor_single_scalar(out=row[0:1, 258:259], in_=row[0:1, 258:259],
                                                               scalar=-8.0, op=ALU.mult), inc=1, waits=[c1])
            c3 = p.emit("pe", lambda o: o.matmul(out=ptf[0:128, 0:1], lhsT=ones1[:], rhs=row[0:1, 258:259],
                                                 start=True, stop=True), inc=1, waits=[c2, o1])
            c4 = p.emit("dv", lambda o: o.tensor_copy(out=nrm[:, 2:3], in_=ptf[0:128, 0:1]), inc=1, waits=[c3])
            hfree = []
            qprev = c4
            for qc in range(8):
                nkt = 4 * qc + 4
                pe_ev = [qprev, qprev]
                mml = mmav = None
                for kt in range(nkt):
                    par = kt % 2
                    mms = p.emit("pe", lambda o, h=h, qc=qc, kt=kt, par=par: o.matmul(
                        out=ps_s[par][0:128, 0:512], lhsT=ksb[:, kt * 128:(kt + 1) * 128],
                        rhs=qsb[:, qc * 512:(qc + 1) * 512], start=True, stop=True),
                        inc=1, waits=[pe_ev[par], c4] if kt < 2 else [pe_ev[par]])
                    exv = p.emit("ac", lambda o, kt=kt, par=par: o.activation(
                        out=Pt[par][:], in_=ps_s[par][0:128, 0:512], func=AF.Exp,
                        bias=nrm[:, 2:3], scale=0.125), inc=1, waits=[mms, pe_ev[par]])
                    dep_p = exv
                    if kt >= 4 * qc:
                        j = kt - 4 * qc
                        dep_p = p.emit("dv", lambda o, j=j, par=par: o.tensor_mul(
                            out=Pt[par][:], in0=Pt[par][:], in1=msk[:, j * 512:(j + 1) * 512]),
                            inc=1, waits=[exv, lmk])
                    mml = p.emit("pe", lambda o, kt=kt, par=par, nkt=nkt: o.matmul(
                        out=ps_l[0:1, 0:512], lhsT=onesb[:], rhs=Pt[par][:],
                        start=(kt == 0), stop=(kt == nkt - 1)),
                        inc=(1 if kt == nkt - 1 else 0),
                        waits=[dep_p, o2] + ([qprev] if kt == 0 else []))
                    mmav = p.emit("pe", lambda o, kt=kt, par=par, nkt=nkt: o.matmul(
                        out=ps_av[0:64, 0:512], lhsT=vsb[:, kt * 64:(kt + 1) * 64], rhs=Pt[par][:],
                        start=(kt == 0), stop=(kt == nkt - 1)),
                        inc=(1 if kt == nkt - 1 else 0), waits=[lv, qprev] if kt == 0 else [])
                    pe_ev[par] = mmav
                el = p.emit("dv", lambda o: o.tensor_copy(out=rl[0:1, 0:512], in_=ps_l[0:1, 0:512]),
                            inc=1, waits=[mml, mmav])
                rc = p.emit("dv", lambda o: o.reciprocal(out=rl[0:1, 0:512], in_=rl[0:1, 0:512]),
                            inc=1, waits=[el])
                mb = p.emit("pe", lambda o: o.matmul(out=ps_b[0:64, 0:512], lhsT=ones1[0:1, 0:64],
                                                     rhs=rl[0:1, 0:512], start=True, stop=True),
                            inc=1, waits=[rc])
                eb = p.emit("dv", lambda o: o.tensor_copy(out=ab[:, 0:512], in_=ps_b[0:64, 0:512]),
                            inc=1, waits=[mb, abfree])
                ea = p.emit("dv", lambda o: o.tensor_copy(out=ab[:, 512:1024], in_=ps_av[0:64, 0:512]),
                            inc=1, waits=[mb])
                ou = p.emit("dv", lambda o: o.tensor_mul(out=ao[:], in0=ab[:, 512:1024], in1=ab[:, 0:512]),
                            inc=1, waits=[eb, ea, aofree])
                abfree = ou
                st = p.dma("gp", lambda o, h=h, qc=qc: o.dma_start(
                    out=APX(a_out, h * 64 * L + qc * 512, [[L, 64], [1, 512]]), in_=ao[:]),
                    waits=[ou])
                aofree = st
                qprev = ou
                hfree.append(mmav)
            kfree = p.tok("pe")
            qfree = kfree
            vfree = kfree
        p.barrier()
        p.finalize(blk)
    return nc



_IDENT = np.eye(128, dtype=np.float32).reshape(-1)


def _masks_bf16():
    import ml_dtypes
    ms = []
    for j in range(4):
        k = np.arange(128)[:, None] + 128 * j
        qq = np.arange(512)[None, :]
        ms.append((k <= qq).astype(np.float32))
    return np.stack(ms).astype(ml_dtypes.bfloat16).reshape(-1)


def _seg():
    e = np.arange(8)[:, None]
    pp = np.arange(128)[None, :]
    return (pp // 16 == e).astype(np.float32)


def kernel(**inputs):
    import ml_dtypes
    W = {k: np.ascontiguousarray(np.asarray(v, dtype=np.float32)) for k, v in inputs.items()}
    X = W["L0_attn_x"]
    cores = list(range(8))

    nc = build_A1()
    maps = [a1_inputs(X[b], W) for b in range(8)]
    r = run_bass_kernel_spmd(nc, maps, core_ids=cores).results
    x1 = np.stack([r[b]["x1"].reshape(L, D) for b in range(8)])
    v0 = np.stack([np.asarray(r[b]["v0"]).reshape(D, L) for b in range(8)])
    g1 = np.stack([r[b]["g1"].reshape(D, L) for b in range(8)])
    g2 = np.stack([r[b]["g2"].reshape(D, L) for b in range(8)])

    nc = build_A2()
    maps = [a2_inputs(c, v0, g1, g2, W) for c in range(8)]
    r = run_bass_kernel_spmd(nc, maps, core_ids=cores).results
    v2 = np.concatenate([np.asarray(r[c]["v2"]).reshape(128, 128, 32, 8) for c in range(8)], axis=0)
    v2_cm = np.ascontiguousarray(v2.transpose(3, 0, 2, 1).reshape(8, 1024, L))

    nc = build_PROJ(True)
    common = {
        "wpf": W["L0_attn_out_proj_w"].reshape(-1),
        "bp": W["L0_attn_out_proj_b"].reshape(-1),
        "wgTf": np.ascontiguousarray(W["L1_ffn_Wg"].T).reshape(-1),
        "identf": _IDENT,
        "w1f": W["L0_ffn_w1"].reshape(-1),
        "w2f": W["L0_ffn_w2"].reshape(-1),
        "gwTf": np.ascontiguousarray(W["L0_ffn_gate_w"].T).reshape(-1),
        "b1r": np.ascontiguousarray(W["L0_ffn_b1"].reshape(8, 16, 128).transpose(2, 0, 1)).reshape(-1),
        "b2": W["L0_ffn_b2"].reshape(-1),
        "gb0": W["L0_ffn_gate_b"].reshape(-1),
        "rmsw": W["L0_ffn_rms_w"].reshape(-1),
    }
    maps = [dict(common, x=x1[b].reshape(-1), vin=v2_cm[b].reshape(-1)) for b in range(8)]
    r = run_bass_kernel_spmd(nc, maps, core_ids=cores).results
    x3 = np.stack([r[b]["xo"].reshape(L, D) for b in range(8)])
    z4 = np.stack([np.asarray(r[b]["zcm"]).reshape(D, L) for b in range(8)])
    sc1 = np.stack([r[b]["sc"].reshape(L, 8) for b in range(8)])

    def grid_of(sc):
        s = sc.reshape(NTOK_ALL, 8).T.reshape(8, 16, 2048).reshape(128, 2048)
        return np.ascontiguousarray(s).reshape(-1)

    nc = build_L1FFN()
    l1common = {
        "grid": grid_of(sc1),
        "seg": _seg().reshape(-1),
        "seg2": np.ascontiguousarray(_seg().T).reshape(-1),
        "identf": _IDENT,
        "w1Tf": np.ascontiguousarray(W["L1_ffn_W1"].transpose(0, 2, 1)).reshape(-1),
        "w3Tf": np.ascontiguousarray(W["L1_ffn_W3"].transpose(0, 2, 1)).reshape(-1),
        "w2Tf": np.ascontiguousarray(W["L1_ffn_W2"].transpose(0, 2, 1)).reshape(-1),
    }
    maps = [dict(l1common, x=x3[b].reshape(-1), zcm=z4[b].reshape(-1),
                 stm=np.ascontiguousarray(sc1[b]).reshape(-1)) for b in range(8)]
    r = run_bass_kernel_spmd(nc, maps, core_ids=cores).results
    x4 = np.stack([r[b]["xo"].reshape(L, D) for b in range(8)])

    nc = build_QKV()
    qkvcommon = {
        "identf": _IDENT,
        "wdqf": W["L1_attn_W_dq"].reshape(-1),
        "wdkf": W["L1_attn_W_dkv"].reshape(-1),
        "wuqf": W["L1_attn_W_uq"].reshape(-1),
        "wukf": W["L1_attn_W_uk"].reshape(-1),
        "wuvf": W["L1_attn_W_uv"].reshape(-1),
    }
    maps = [dict(qkvcommon, x=x4[b].reshape(-1)) for b in range(8)]
    r = run_bass_kernel_spmd(nc, maps, core_ids=cores).results
    Q = [np.asarray(r[b]["q"]) for b in range(8)]
    Kk = [np.asarray(r[b]["k"]) for b in range(8)]
    Vv = [np.asarray(r[b]["v"]) for b in range(8)]

    nc = build_ATTN()
    mb = _masks_bf16()
    maps = [{"q": Q[b], "k": Kk[b], "v": Vv[b], "identf": _IDENT, "maskb": mb} for b in range(8)]
    r = run_bass_kernel_spmd(nc, maps, core_ids=cores).results
    attn = [np.asarray(r[b]["attn"]) for b in range(8)]

    nc = build_PROJ(False)
    zeros1024 = np.zeros(D, np.float32)
    common2 = {
        "wpf": W["L1_attn_W_o"].reshape(-1),
        "bp": zeros1024,
        "wgTf": np.ascontiguousarray(W["L1_ffn_Wg"].T).reshape(-1),
        "identf": _IDENT,
        "w1f": W["L0_ffn_w1"].reshape(-1),
        "w2f": W["L0_ffn_w2"].reshape(-1),
        "gwTf": np.ascontiguousarray(W["L0_ffn_gate_w"].T).reshape(-1),
        "b1r": common["b1r"],
        "b2": W["L0_ffn_b2"].reshape(-1),
        "gb0": W["L0_ffn_gate_b"].reshape(-1),
        "rmsw": W["L0_ffn_rms_w"].reshape(-1),
    }
    maps = [dict(common2, x=x4[b].reshape(-1), vin=attn[b]) for b in range(8)]
    r = run_bass_kernel_spmd(nc, maps, core_ids=cores).results
    x5 = np.stack([r[b]["xo"].reshape(L, D) for b in range(8)])
    z6 = np.stack([np.asarray(r[b]["zcm"]).reshape(D, L) for b in range(8)])
    sc2 = np.stack([r[b]["sc"].reshape(L, 8) for b in range(8)])

    nc = build_L1FFN()
    l1common["grid"] = grid_of(sc2)
    maps = [dict(l1common, x=x5[b].reshape(-1), zcm=z6[b].reshape(-1),
                 stm=np.ascontiguousarray(sc2[b]).reshape(-1)) for b in range(8)]
    r = run_bass_kernel_spmd(nc, maps, core_ids=cores).results
    out = np.stack([r[b]["xo"].reshape(L, D) for b in range(8)])
    return out.astype(np.float32)


NTOK_ALL = B * L
